# version 01
# baseline (speedup 1.0000x reference)
import math
import numpy as np
import ml_dtypes
from contextlib import ExitStack

import concourse.bass as bass
import concourse.mybir as mybir
from concourse.bass_utils import run_bass_kernel_spmd

F32 = mybir.dt.float32
BF16 = mybir.dt.bfloat16
FP8 = mybir.dt.float8e4
AF = mybir.ActivationFunctionType
ALU = mybir.AluOpType
AX = mybir.AxisListType

NQ = 8192
DM = 2048
NT = 1024
NH = 16
HD = 128
WIN = 512
SINKS = 4
NCORE = 8
CSH = 20.0
ASCALE = 1.0 / math.sqrt(HD)
NEG = -1.0e30
EPS = 1e-5
L3_DEBUG = False
ENGS = ("sync", "vector", "scalar", "tensor", "gpsimd")


class Sem:
    def __init__(self, name):
        self.name = name
        self.v = 0
        self.h = None

    def bump(self, n):
        self.v += n
        return self.v


class Prog:

    def __init__(self):
        self.ops = {e: [] for e in ENGS}
        self.sems = []

    def sem(self, name):
        s = Sem(name)
        self.sems.append(s)
        return s

    def add(self, eng, fn, waits=(), inc=None, inc_n=1):
        planned = None
        if inc is not None:
            planned = inc.bump(inc_n)
        self.ops[eng].append((fn, list(waits), inc, inc_n))
        return planned

    def dma(self, eng, out, in_, waits=(), inc=None, **kw):
        return self.add(eng, lambda e: e.dma_start(out=out, in_=in_, **kw),
                        waits=waits, inc=inc, inc_n=16)

    def emit(self, nc, ctx):
        for s in self.sems:
            s.h = ctx.enter_context(nc.semaphore(s.name))
        block = ctx.enter_context(nc.Block())

        def mk(eng_name):
            def body(eng):
                for fn, waits, inc, inc_n in self.ops[eng_name]:
                    for (s, v) in waits:
                        eng.wait_ge(s.h, v)
                    instr = fn(eng)
                    if inc is not None:
                        assert instr is not None
                        instr.then_inc(inc.h, inc_n)
            return body

        for name in ENGS:
            if not self.ops[name]:
                continue
            getattr(block, name)(mk(name))


def build_l1():
    nc = bass.Bass("TRN2", target_bir_lowering=False)
    x_sl = nc.dram_tensor("x_sl", [DM, NT], F32, kind="ExternalInput").ap()
    c_in = nc.dram_tensor("c_in", [1, DM], F32, kind="ExternalInput").ap()
    wmod_sl = nc.dram_tensor("wmod_sl", [DM, 1536], F32, kind="ExternalInput").ap()
    wfn = nc.dram_tensor("wfn", [256, DM], F32, kind="ExternalInput").ap()
    wft = nc.dram_tensor("wft", [DM, 256], F32, kind="ExternalInput").ap()
    ident = nc.dram_tensor("ident", [128, 128], F32, kind="ExternalInput").ap()
    o_stats = nc.dram_tensor("o_stats", [2, NT], F32, kind="ExternalOutput").ap()
    o_mod = nc.dram_tensor("o_mod", [128, 12], F32, kind="ExternalOutput").ap()
    o_wdq = nc.dram_tensor("o_wdq", [DM, 256], BF16, kind="ExternalOutput").ap()
    scr_rs = nc.dram_tensor("scr_rs", [16, 256], F32, kind="Internal").ap()
    scr_s2 = nc.dram_tensor("scr_s2", [16, 256], F32, kind="Internal").ap()

    p = Prog()
    ld = p.sem("ld")
    sil = p.sem("sil")
    wld = p.sem("wld")
    xld = p.sem("xld")
    cstv = p.sem("cstv")
    csts = p.sem("csts")
    smm = p.sem("smm")
    mmm = p.sem("mmm")
    ved = p.sem("ved")
    sca = p.sem("sca")
    bcd = p.sem("bcd")
    qd = p.sem("qd")
    st = p.sem("st")

    with ExitStack() as ctx:
        def sbt(name, shape, dtype=F32):
            return ctx.enter_context(nc.sbuf_tensor(name, shape, dtype))
        cfm = sbt("cfm", [128, 16])
        scfm = sbt("scfm", [128, 16])
        wmodb = [sbt(f"wmodb{i}", [128, 1536]) for i in range(2)]
        x32 = [sbt(f"x32_{i}", [128, NT]) for i in range(2)]
        xbb = [sbt(f"xbb{i}", [128, NT], BF16) for i in range(2)]
        sqb = [sbt(f"sqb{i}", [128, NT], BF16) for i in range(2)]
        wfns = sbt("wfns", [128, 2 * DM])
        wfts = sbt("wfts", [128, 16 * 256])
        idt = sbt("idt", [128, 128])
        modsb = sbt("modsb", [128, 12])
        ones_bf = sbt("ones_bf", [128, 1], BF16)
        t_mean = sbt("t_mean", [1, NT])
        t_var = sbt("t_var", [1, NT])
        t_m2 = sbt("t_m2", [1, NT])
        amx = sbt("amx", [128, 32])
        sT = sbt("sT", [16, 256])
        rs2T = sbt("rs2T", [16, 256])
        s2T = sbt("s2T", [16, 256])
        repA = [sbt(f"repA{i}", [128, 256]) for i in range(2)]
        repB = [sbt(f"repB{i}", [128, 256]) for i in range(2)]
        yq = [sbt(f"yq{i}", [128, 256]) for i in range(2)]
        q8 = [sbt(f"q8_{i}", [128, 256], FP8) for i in range(2)]
        wdq = sbt("wdq", [128, 16 * 256], BF16)
        epsb = sbt("epsb", [1, 1])

        psmod = ctx.enter_context(nc.psum_tensor("psmod", [128, 12], F32))
        ps_sa = ctx.enter_context(nc.psum_tensor("ps_sa", [1, 512], F32))
        ps_sb = ctx.enter_context(nc.psum_tensor("ps_sb", [1, 512], F32))
        ps_qa = ctx.enter_context(nc.psum_tensor("ps_qa", [1, 512], F32))
        ps_qb = ctx.enter_context(nc.psum_tensor("ps_qb", [1, 512], F32))
        ps_t = ctx.enter_context(nc.psum_tensor("ps_t", [16, 128], F32))

        p.dma("sync", cfm[:], c_in.rearrange("o (k q) -> q (o k)", q=128), inc=ld,
              allow_slow_non_contiguous=True)
        p.dma("sync", idt[:], ident, inc=ld)
        p.dma("sync", wfns.ap().rearrange("q (a n) -> q a n", a=2),
              wfn.rearrange("(a q) n -> q a n", q=128), inc=ld)
        p.dma("sync", wfts.ap().rearrange("q (k n) -> q k n", k=16),
              wft.rearrange("(k q) n -> q k n", q=128), inc=ld)
        ld_all = ld.v

        p.add("scalar", lambda e: e.activation(out=scfm[:], in_=cfm[:], func=AF.Silu),
              waits=[(ld, ld_all)], inc=sil)

        for K in range(16):
            p.dma("sync", wmodb[K % 2][:], wmod_sl[128 * K:128 * K + 128, :],
                  waits=([(mmm, (K - 2) * 12 + 12)] if K >= 2 else []), inc=wld)
            for oc in range(12):
                p.add("tensor",
                      (lambda K=K, oc=oc: lambda e: e.matmul(
                          psmod[:, oc:oc + 1],
                          wmodb[K % 2][:, 128 * oc:128 * oc + 128],
                          scfm[:, K:K + 1],
                          start=(K == 0 and oc == 0),
                          stop=(K == 15 and oc == 11)))(),
                      waits=([(wld, 16 * (K + 1)), (sil, 1)] if oc == 0 else []),
                      inc=mmm)
        p.add("vector", lambda e: e.tensor_copy(modsb[:], psmod[:]),
              waits=[(mmm, mmm.v)], inc=ved)
        p.dma("gpsimd", o_mod, modsb[:], waits=[(ved, ved.v)], inc=st)

        p.add("vector", lambda e: e.memset(ones_bf[:], 1.0), inc=ved)
        ones_ready = ved.v
        for K in range(16):
            w = [(cstv, K - 1), (csts, K - 1)] if K >= 2 else []
            p.dma("sync", x32[K % 2][:], x_sl[128 * K:128 * K + 128, :],
                  waits=w, inc=xld)
            wr = [(smm, 4 * (K - 1))] if K >= 2 else []
            p.add("vector",
                  (lambda K=K: lambda e: e.tensor_copy(xbb[K % 2][:], x32[K % 2][:]))(),
                  waits=[(xld, 16 * (K + 1))] + wr, inc=cstv)
            p.add("scalar",
                  (lambda K=K: lambda e: e.activation(
                      out=sqb[K % 2][:], in_=x32[K % 2][:], func=AF.Square))(),
                  waits=[(xld, 16 * (K + 1))] + wr, inc=csts)
            for h in range(2):
                p.add("tensor",
                      (lambda K=K, h=h: lambda e: e.matmul(
                          (ps_sa if h == 0 else ps_sb)[:],
                          ones_bf[:], xbb[K % 2][:, 512 * h:512 * h + 512],
                          start=(K == 0), stop=(K == 15)))(),
                      waits=[(cstv, K + 1), (ved, ones_ready)], inc=smm)
                p.add("tensor",
                      (lambda K=K, h=h: lambda e: e.matmul(
                          (ps_qa if h == 0 else ps_qb)[:],
                          ones_bf[:], sqb[K % 2][:, 512 * h:512 * h + 512],
                          start=(K == 0), stop=(K == 15)))(),
                      waits=[(csts, K + 1)], inc=smm)
        smm_all = smm.v
        for h, (pss, psq) in enumerate(((ps_sa, ps_qa), (ps_sb, ps_qb))):
            sl = slice(512 * h, 512 * h + 512)
            p.add("vector",
                  (lambda pss=pss, sl=sl: lambda e: e.tensor_scalar(
                      t_mean[0:1, sl], pss[:], 1.0 / DM, None, op0=ALU.mult))(),
                  waits=[(smm, smm_all)], inc=ved)
            p.add("vector",
                  (lambda psq=psq, sl=sl: lambda e: e.tensor_scalar(
                      t_var[0:1, sl], psq[:], 1.0 / DM, None, op0=ALU.mult))(),
                  inc=ved)
            p.add("vector",
                  (lambda sl=sl: lambda e: e.tensor_mul(
                      t_m2[0:1, sl], t_mean[0:1, sl], t_mean[0:1, sl]))(),
                  inc=ved)
            p.add("vector",
                  (lambda sl=sl: lambda e: e.tensor_sub(
                      t_var[0:1, sl], t_var[0:1, sl], t_m2[0:1, sl]))(),
                  inc=ved)
        p.dma("gpsimd", o_stats[0:1, :], t_mean[:], waits=[(ved, ved.v)], inc=st)
        p.dma("gpsimd", o_stats[1:2, :], t_var[:], inc=st)

        for i in range(2):
            p.add("vector",
                  (lambda i=i: lambda e: e.tensor_reduce(
                      amx[:, 16 * i:16 * i + 16],
                      wfns.ap().rearrange("q (a k f) -> q (a k) f", a=2, k=16)
                      [:, 16 * i:16 * i + 16, :],
                      axis=AX.X, op=ALU.max, apply_absolute_value=True))(),
                  waits=[(ld, ld_all)], inc=ved)
        amx_done = ved.v
        for i in range(2):
            p.add("tensor",
                  (lambda i=i: lambda e: e.transpose(
                      ps_t[:], amx[:, 16 * i:16 * i + 16], idt[:]))(),
                  waits=[(ved, amx_done), (ld, ld_all)] + ([(ved, ved.v)] if i else []),
                  inc=mmm)
            p.add("vector",
                  (lambda i=i: lambda e: e.tensor_scalar(
                      sT[:, 128 * i:128 * i + 128], ps_t[:], 1.0 / 448.0, None,
                      op0=ALU.mult))(),
                  waits=[(mmm, mmm.v)], inc=ved)
        p.add("vector", lambda e: e.tensor_scalar(s2T[:], sT[:], 2.0, None,
                                                  op0=ALU.mult), inc=ved)
        p.add("scalar", lambda e: e.activation(out=sT[:], in_=s2T[:],
                                               func=AF.Ln),
              waits=[(ved, ved.v)], inc=ved)
        p.add("scalar", lambda e: e.activation(out=rs2T[:], in_=sT[:],
                                               func=AF.Exp, scale=-1.0),
              inc=ved)
        sc_done = ved.v
        p.dma("gpsimd", scr_rs, rs2T[:], waits=[(ved, sc_done)], inc=sca)
        p.dma("gpsimd", scr_s2, s2T[:], inc=sca)
        sca_all = sca.v

        for K in range(16):
            i = K % 2
            w = [(sca, sca_all)] + ([(qd, (K - 2) + 1)] if K >= 2 else [])
            p.dma("sync", repA[i][:], scr_rs[K:K + 1, :].partition_broadcast(128),
                  waits=w, inc=bcd)
            p.dma("sync", repB[i][:], scr_s2[K:K + 1, :].partition_broadcast(128),
                  inc=bcd)
            bc_here = bcd.v
            p.add("vector",
                  (lambda K=K, i=i: lambda e: e.tensor_mul(
                      yq[i][:], wfts[:, 256 * K:256 * K + 256], repA[i][:]))(),
                  waits=[(bcd, bc_here), (ld, ld_all)], inc=None)
            p.add("vector",
                  (lambda i=i: lambda e: e.tensor_copy(q8[i][:], yq[i][:]))())
            p.add("vector",
                  (lambda K=K, i=i: lambda e: e.tensor_mul(
                      wdq[:, 256 * K:256 * K + 256], q8[i][:], repB[i][:]))(),
                  inc=qd)
        p.dma("gpsimd", o_wdq.rearrange("(k q) n -> q k n", q=128),
              wdq.ap().rearrange("q (k n) -> q k n", k=16),
              waits=[(qd, qd.v)], inc=st)
        st_all = st.v
        p.add("gpsimd", lambda e: None if e.wait_ge(st.h, st_all) else None)

        p.emit(nc, ctx)
    return nc


def build_l2():
    nc = bass.Bass("TRN2", target_bir_lowering=False)
    x_fm = nc.dram_tensor("x_fm", [DM, NQ], F32, kind="ExternalInput").ap()
    wq_t = nc.dram_tensor("wq_t", [DM, 256], F32, kind="ExternalInput").ap()
    wk_t = nc.dram_tensor("wk_t", [DM, 256], F32, kind="ExternalInput").ap()
    wv_t = nc.dram_tensor("wv_t", [DM, 256], F32, kind="ExternalInput").ap()
    b_pm = nc.dram_tensor("b_pm", [128, 6], F32, kind="ExternalInput").ap()
    mod_all = nc.dram_tensor("mod_all", [128, 96], F32, kind="ExternalInput").ap()
    m8 = nc.dram_tensor("m8", [8, NT], F32, kind="ExternalInput").ap()
    v8 = nc.dram_tensor("v8", [8, NT], F32, kind="ExternalInput").ap()
    tri_in = nc.dram_tensor("tri_in", [128, 128], F32, kind="ExternalInput").ap()
    tris_in = nc.dram_tensor("tris_in", [128, 128], F32, kind="ExternalInput").ap()
    bc_in = nc.dram_tensor("bc_in", [128, 1], F32, kind="ExternalInput").ap()
    bcs_in = nc.dram_tensor("bcs_in", [128, 1], F32, kind="ExternalInput").ap()
    o_raw = nc.dram_tensor("o_raw", [256, NQ], BF16, kind="ExternalOutput").ap()
    o_invd = nc.dram_tensor("o_invd", [2, NQ], F32, kind="ExternalOutput").ap()
    scr_r = nc.dram_tensor("scr_r", [8, NT], F32, kind="Internal").ap()
    scr_mrn = nc.dram_tensor("scr_mrn", [8, NT], F32, kind="Internal").ap()
    ws_d = [nc.dram_tensor(f"ws_d{m}", [16, 128, 256], BF16, kind="Internal").ap()
            for m in range(3)]

    p = Prog()
    ld = p.sem("ld")
    prp = p.sem("prp")
    wl = p.sem("wl")
    wsd = p.sem("wsd")
    uvm = p.sem("uvm")
    rup = p.sem("rup")
    bcd = p.sem("bcd")
    xl = p.sem("xl")
    csd = p.sem("csd")
    wgl = p.sem("wgl")
    qmm = p.sem("qmm")
    epd = p.sem("epd")
    epa = p.sem("epa")
    vtd = p.sem("vtd")
    smm = p.sem("smm")
    trid = p.sem("trid")
    exd = p.sem("exd")
    pvm = p.sem("pvm")
    dnv = p.sem("dnv")
    oev = p.sem("oev")
    ost = p.sem("ost")
    dnm = p.sem("dnm")
    st = p.sem("st")

    with ExitStack() as ctx:
        def sbt(name, shape, dtype=F32):
            return ctx.enter_context(nc.sbuf_tensor(name, shape, dtype))
        q_sb = [sbt(f"q_sb{h}", [128, NQ], BF16) for h in range(2)]
        k_sb = [sbt(f"k_sb{h}", [128, NQ], BF16) for h in range(2)]
        v_tm = [sbt(f"v_tm{h}", [128, NQ], BF16) for h in range(2)]
        ones_f32 = sbt("ones_f32", [128, 1])
        dd_c = sbt("dd_c", [1, NT])
        invd_c = sbt("invd_c", [1, NT])
        xb = [sbt(f"xb{k}", [128, NT], BF16) for k in range(16)]
        x32 = [sbt(f"x32_{i}", [128, NT]) for i in range(2)]
        r_rep = [sbt(f"r_rep{i}", [128, NT]) for i in range(2)]
        mrn_rep = [sbt(f"mrn_rep{i}", [128, NT]) for i in range(2)]
        wgrp = [sbt(f"wgrp{i}", [128, 16 * 128], BF16) for i in range(2)]
        w32 = [sbt(f"w32_{i}", [128, 256]) for i in range(2)]
        wsc = [sbt(f"wsc{i}", [128, 256], BF16) for i in range(2)]
        ep_t = [sbt(f"ep_t{i}", [128, NT]) for i in range(2)]
        vtmp = [sbt(f"vtmp{i}", [128, NT], BF16) for i in range(2)]
        ost_b = [sbt(f"ost_b{i}", [128, NT], BF16) for i in range(2)]
        p_bf = [sbt(f"p_bf{i}", [128, NT], BF16) for i in range(2)]
        pair = sbt("pair", [128, NT], BF16)
        acc = sbt("acc", [128, NT])
        mod_sb = sbt("mod_sb", [128, 96])
        onepca = sbt("onepca", [128, 16])
        one_t = sbt("one_t", [128, 1])
        b_sb = sbt("b_sb", [128, 6])
        u_sb = sbt("u_sb", [128, 6])
        v_sb = sbt("v_sb", [128, 6])
        tri = sbt("tri", [128, 128])
        tris = sbt("tris", [128, 128])
        biasC = sbt("biasC", [128, 1])
        biasCs = sbt("biasCs", [128, 1])
        m8s = sbt("m8s", [8, 512])
        v8s = sbt("v8s", [8, 512])
        r8x = sbt("r8x", [8, 512])
        epsb = sbt("epsb", [8, 1])
        ones_bf = sbt("ones_bf", [128, 1], BF16)

        S0 = ctx.enter_context(nc.psum_tensor("S0", [128, NT], F32))
        S1 = ctx.enter_context(nc.psum_tensor("S1", [128, NT], F32))
        O0 = ctx.enter_context(nc.psum_tensor("O0", [128, NT], F32))
        O1 = ctx.enter_context(nc.psum_tensor("O1", [128, NT], F32))
        SP = [S0, S1]
        OP = [O0, O1]

        p.dma("sync", mod_sb[:], mod_all, inc=ld)
        p.dma("sync", b_sb[:], b_pm, inc=ld)
        p.dma("sync", tri[:], tri_in, inc=ld)
        p.dma("sync", tris[:], tris_in, inc=ld)
        p.dma("sync", biasC[:], bc_in, inc=ld)
        p.dma("sync", biasCs[:], bcs_in, inc=ld)
        ld_all = ld.v
        p.add("vector", lambda e: e.memset(ones_bf[:], 1.0), inc=prp)
        p.add("vector", lambda e: e.memset(ones_f32[:], 1.0), inc=prp)
        p.add("vector", lambda e: e.memset(epsb[:], EPS), inc=prp)
        p.add("vector", lambda e: e.memset(ones_bf[:], 1.0), inc=prp)
        p.add("vector", lambda e: e.memset(one_t[:], 1.0), inc=prp)
        p.add("scalar", lambda e: e.activation(
            out=onepca[:], in_=mod_sb[:, 16:32], func=AF.Identity,
            bias=one_t[:]),
            waits=[(ld, ld_all), (prp, prp.v)], inc=prp)
        onepca_ready = prp.v
        for hh in range(2):
            hsl = slice(512 * hh, 512 * hh + 512)
            p.dma("sync", m8s[:], m8[:, hsl], waits=[(rup, rup.v)], inc=ld)
            p.dma("sync", v8s[:], v8[:, hsl], inc=ld)
            p.add("scalar",
                  (lambda: lambda e: e.activation(
                      out=v8s[:], in_=v8s[:], func=AF.Ln, bias=epsb[:]))(),
                  waits=[(ld, ld.v), (prp, prp.v)], inc=prp)
            p.add("vector",
                  (lambda: lambda e: e.tensor_scalar(
                      m8s[:], m8s[:], -1.0, None, op0=ALU.mult))(),
                  waits=[(ld, ld.v)], inc=prp)
            p.add("scalar",
                  (lambda: lambda e: e.activation(
                      out=r8x[:], in_=v8s[:], func=AF.Exp, scale=-0.5))(),
                  waits=[(prp, prp.v)], inc=prp)
            p.add("gpsimd",
                  (lambda: lambda e: e.tensor_mul(m8s[:], m8s[:], r8x[:]))(),
                  waits=[(prp, prp.v)], inc=prp)
            p.dma("gpsimd", scr_r[:, hsl], r8x[:], waits=[(prp, prp.v)], inc=rup)
            p.dma("gpsimd", scr_mrn[:, hsl], m8s[:], inc=rup)
        rup_all = rup.v

        wts = [wq_t, wk_t, wv_t]
        first_uv = True
        for m in range(3):
            for f in range(16):
                i = (m * 16 + f) % 2
                wwait = ([(uvm, uvm.v), (wsd, wsd.v)] if (m * 16 + f) >= 2 else [])
                p.dma("sync", w32[i][:], wts[m][128 * f:128 * f + 128, :],
                      waits=wwait, inc=wl)
                p.add("vector",
                      (lambda i=i, f=f: lambda e: e.tensor_scalar(
                          wsc[i][:], w32[i][:], onepca[:, f:f + 1], None,
                          op0=ALU.mult))(),
                      waits=[(wl, wl.v), (prp, onepca_ready)], inc=prp)
                p.dma("gpsimd", ws_d[m].rearrange("a p n -> p a n")[:, f, :],
                      wsc[i].ap(), waits=[(prp, prp.v)], inc=wsd)
                for oc in range(2):
                    col = 2 * m + oc
                    p.add("tensor",
                          (lambda i=i, col=col, oc=oc, fu=first_uv,
                           lst=(m == 2 and f == 15 and oc == 1): lambda e: e.matmul(
                              S0[:, col:col + 1],
                              wsc[i][:, 128 * oc:128 * oc + 128], ones_bf[:],
                              start=fu, stop=lst))(),
                          waits=[(prp, prp.v)], inc=uvm)
                    first_uv = False
                    p.add("tensor",
                          (lambda i=i, col=col, oc=oc, m=m, f=f: lambda e: e.matmul(
                              S1[:, col:col + 1],
                              w32[i][:, 128 * oc:128 * oc + 128],
                              mod_sb[:, f:f + 1],
                              start=(m == 0 and f == 0 and oc == 0),
                              stop=(m == 2 and f == 15 and oc == 1)))(),
                          waits=[], inc=uvm)
        uv_all = uvm.v
        p.add("vector", lambda e: e.tensor_copy(u_sb[:], S0[:, 0:6]),
              waits=[(uvm, uv_all)], inc=prp)
        p.add("vector", lambda e: e.tensor_add(v_sb[:], S1[:, 0:6], b_sb[:]),
              inc=prp)
        uv_done = prp.v

        xb_free = [0] * 16
        x32_free = [0] * 2
        r_free = [0, 0]
        mrn_free = [0, 0]
        wg_free = [0, 0]
        ps_free = [0, 0, 0, 0]
        ep_free = [0, 0]
        vt_free = [0, 0]
        ost_free = [0, 0]
        for T in range(8):
            tp = T % 2
            p.dma("gpsimd", r_rep[tp][:],
                  scr_r[T:T + 1, :].partition_broadcast(128),
                  waits=[(rup, rup_all), (csd, r_free[tp])], inc=bcd)
            p.dma("gpsimd", mrn_rep[tp][:],
                  scr_mrn[T:T + 1, :].partition_broadcast(128),
                  waits=[(epd, mrn_free[tp])], inc=bcd)
            bc_here = bcd.v
            for K in range(16):
                xi = (T * 16 + K) % 2
                p.dma("sync", x32[xi][:],
                      x_fm.rearrange("(k q) n -> q k n", q=128)
                      [:, K, NT * T:NT * T + NT],
                      waits=[(csd, x32_free[xi])], inc=xl)
                p.add("vector",
                      (lambda K=K, xi=xi, tp=tp: lambda e: e.tensor_tensor(
                          xb[K][:], x32[xi][:], r_rep[tp][:], op=ALU.mult))(),
                      waits=[(xl, xl.v), (bcd, bc_here - 16),
                             (qmm, xb_free[K]), (prp, uv_done)], inc=csd)
                x32_free[xi] = csd.v
            csd_T = csd.v
            r_free[tp] = csd_T
            for o6 in range(6):
                m, oc = divmod(o6, 2)
                gi = o6 % 2
                psi = o6 % 4
                ps = (S0, S1, O0, O1)[psi]
                p.dma("sync", wgrp[gi].ap().rearrange("q (a n) -> q a n", a=16),
                      ws_d[m].rearrange("a p n -> p a n")
                      [:, :, 128 * oc:128 * oc + 128],
                      waits=[(wsd, wsd.v), (qmm, wg_free[gi])], inc=wgl)
                wg_here = wgl.v
                for f in range(16):
                    for half in range(2):
                        p.add("tensor",
                              (lambda ps=ps, f=f, half=half, gi=gi: lambda e:
                               e.matmul(
                                  ps[:, 512 * half:512 * half + 512],
                                  wgrp[gi][:, 128 * f:128 * f + 128],
                                  xb[f][:, 512 * half:512 * half + 512],
                                  start=(f == 0), stop=(f == 15)))(),
                              waits=([(csd, csd_T), (wgl, wg_here),
                                      (epd, ps_free[psi])]
                                     if f == 0 and half == 0 else []),
                              inc=qmm)
                wg_free[gi] = qmm.v
                for k in range(16):
                    xb_free[k] = qmm.v
                ei = o6 % 2
                p.add("vector",
                      (lambda ps=ps, ei=ei, tp=tp, col=o6: lambda e:
                       e.scalar_tensor_tensor(
                          ep_t[ei][:], in0=mrn_rep[tp][:],
                          scalar=u_sb[:, col:col + 1],
                          in1=ps[:], op0=ALU.mult, op1=ALU.add))(),
                      waits=[(qmm, qmm.v), (epa, ep_free[ei]),
                             (bcd, bc_here)], inc=epd)
                ps_free[psi] = epd.v
                if m < 2:
                    dst = (q_sb if m == 0 else k_sb)[oc]
                    p.add("scalar",
                          (lambda dst=dst, ei=ei, T=T, col=o6: lambda e:
                           e.activation(
                              out=dst[:, NT * T:NT * T + NT], in_=ep_t[ei][:],
                              func=AF.Identity, bias=v_sb[:, col:col + 1]))(),
                          waits=[(epd, epd.v)], inc=epa)
                    ep_free[ei] = epa.v
                else:
                    vi = oc
                    p.add("scalar",
                          (lambda ei=ei, vi=vi, col=o6: lambda e: e.activation(
                              out=vtmp[vi][:], in_=ep_t[ei][:],
                              func=AF.Identity, bias=v_sb[:, col:col + 1]))(),
                          waits=[(epd, epd.v), (vtd, vt_free[vi])], inc=epa)
                    ep_free[ei] = epa.v
                    for j in range(8):
                        p.dma("scalar",
                              v_tm[oc][:, NT * T + 128 * j:NT * T + 128 * j + 128],
                              vtmp[vi][:, 128 * j:128 * j + 128],
                              waits=([(epa, epa.v)] if j == 0 else []), inc=vtd,
                              transpose=True)
                    vt_free[vi] = vtd.v
            mrn_free[tp] = epd.v
        qkv_done_act = epa.v
        qkv_done_vt = vtd.v

        s_free = [0, 0]
        pbf_free = [(0, 0), (0, 0)]
        o_ps_free = [0, 0]
        for h in range(2):
            head_gate = dnv.v
            for t in range(8):
                i0 = NT * t
                bmin = max(0, 8 * t - 4)
                bs = list(range(63, bmin - 1, -1))
                if bmin > 0:
                    bs.append(0)
                p.add("vector", lambda e: e.memset(acc[:], 0.0),
                      waits=[(dnv, dnv.v)], inc=dnv)
                nb = len(bs)
                for idx, b in enumerate(bs):
                    sp = idx % 2
                    is_sink = (b == 0 and bmin > 0)
                    is_t0b0 = (t == 0 and b == 0)
                    if is_sink or is_t0b0:
                        npr = NT
                    else:
                        npr = min(NT, 128 * (b + 5) - i0)
                    crossing = (not is_sink) and (128 * (b + 4) - i0 < NT)
                    pieces = [(j * 512, min(512, npr - j * 512))
                              for j in range((npr + 511) // 512)]
                    for pi, (c0, cw) in enumerate(pieces):
                        p.add("tensor",
                              (lambda sp=sp, h=h, b=b, i0=i0, c0=c0, cw=cw:
                               lambda e: e.matmul(
                                  SP[sp][:, c0:c0 + cw],
                                  k_sb[h][:, 128 * b:128 * b + 128],
                                  q_sb[h][:, i0 + c0:i0 + c0 + cw],
                                  start=True, stop=True))(),
                              waits=([(exd, s_free[sp]), (epa, qkv_done_act),
                                      (dnv, head_gate)]
                                     if pi == 0 else []),
                              inc=smm)
                    if crossing:
                        col0 = 128 * (b + 4) - i0
                        mt = tris if b == 0 else tri
                        p.add("vector",
                              (lambda sp=sp, col0=col0, mt=mt: lambda e:
                               e.tensor_add(
                                  SP[sp][:, col0:col0 + 128],
                                  SP[sp][:, col0:col0 + 128], mt[:]))(),
                              waits=[(smm, smm.v)], inc=trid)
                    ewaits = ([(trid, trid.v)] if crossing else [(smm, smm.v)])
                    ewaits += [(pvm, pbf_free[sp][0]), (dnv, pbf_free[sp][1])]
                    if is_t0b0:
                        p.add("scalar",
                              (lambda sp=sp: lambda e: e.activation(
                                  out=p_bf[sp][:, 0:640], in_=SP[sp][:, 0:640],
                                  func=AF.Exp, bias=biasC[:], scale=ASCALE))(),
                              waits=ewaits)
                        p.add("scalar",
                              (lambda sp=sp: lambda e: e.activation(
                                  out=p_bf[sp][:, 640:NT], in_=SP[sp][:, 640:NT],
                                  func=AF.Exp, bias=biasCs[:], scale=ASCALE))(),
                              inc=exd)
                    else:
                        bias_t = biasCs if is_sink else biasC
                        p.add("scalar",
                              (lambda sp=sp, npr=npr, bias_t=bias_t: lambda e:
                               e.activation(
                                  out=p_bf[sp][:, 0:npr], in_=SP[sp][:, 0:npr],
                                  func=AF.Exp, bias=bias_t[:], scale=ASCALE))(),
                              waits=ewaits, inc=exd)
                    s_free[sp] = exd.v
                    if npr < NT:
                        p.add("vector",
                              (lambda sp=sp, npr=npr: lambda e: e.memset(
                                  p_bf[sp][:, npr:NT], 0.0))(),
                              waits=[(exd, exd.v)], inc=dnv)
                    op_ = t % 2
                    for pi, (c0, cw) in enumerate(pieces):
                        p.add("tensor",
                              (lambda op_=op_, h=h, b=b, sp=sp, c0=c0, cw=cw,
                               stt=(idx == 0),
                               stp=(idx == nb - 1 and c0 + cw >= NT): lambda e:
                               e.matmul(
                                  OP[op_][:, c0:c0 + cw],
                                  v_tm[h][:, 128 * b:128 * b + 128],
                                  p_bf[sp][:, c0:c0 + cw],
                                  start=stt, stop=stp))(),
                              waits=(([(exd, exd.v), (vtd, qkv_done_vt)] +
                                      ([(dnv, o_ps_free[op_])] if idx == 0
                                       else []))
                                     if pi == 0 else []),
                              inc=pvm)
                    if idx % 2 == 1:
                        p.add("vector",
                              lambda e: e.tensor_add(
                                  pair[:], p_bf[0][:], p_bf[1][:]),
                              waits=[(exd, exd.v), (dnv, dnv.v)], inc=dnv)
                        p.add("vector",
                              lambda e: e.tensor_add(acc[:], acc[:], pair[:]),
                              inc=dnv)
                        pbf_free[0] = (pvm.v, dnv.v)
                        pbf_free[1] = (pvm.v, dnv.v)
                    elif idx == nb - 1:
                        p.add("vector",
                              (lambda sp=sp: lambda e: e.tensor_add(
                                  acc[:], acc[:], p_bf[sp][:]))(),
                              waits=[(exd, exd.v), (dnv, dnv.v)], inc=dnv)
                        pbf_free[sp] = (pvm.v, dnv.v)
                op_ = t % 2
                p.add("vector",
                      (lambda op_=op_: lambda e: e.tensor_copy(
                          ost_b[op_][:], OP[op_][:]))(),
                      waits=[(pvm, pvm.v), (ost, ost_free[op_])], inc=oev)
                p.dma("gpsimd", o_raw[128 * h:128 * h + 128, i0:i0 + NT],
                      ost_b[op_].ap(), waits=[(oev, oev.v)], inc=ost)
                ost_free[op_] = ost.v
                for half in range(2):
                    p.add("tensor",
                          (lambda op_=op_, half=half: lambda e: e.matmul(
                              OP[op_][0:1, 512 * half:512 * half + 512],
                              ones_f32[:],
                              acc[:, 512 * half:512 * half + 512],
                              start=True, stop=True))(),
                          waits=([(oev, oev.v), (dnv, dnv.v)]
                                 if half == 0 else []),
                          inc=dnm)
                p.add("vector",
                      (lambda op_=op_: lambda e: e.tensor_copy(
                          dd_c[:], OP[op_][0:1, :]))(),
                      waits=[(dnm, dnm.v)], inc=dnv)
                p.add("scalar", lambda e: e.activation(
                    out=dd_c[:], in_=dd_c[:], func=AF.Ln),
                    waits=[(dnv, dnv.v)], inc=dnv)
                p.add("scalar", lambda e: e.activation(
                    out=invd_c[:], in_=dd_c[:], func=AF.Exp, scale=-1.0),
                    inc=dnv)
                p.dma("gpsimd", o_invd[h:h + 1, i0:i0 + NT], invd_c[:],
                      waits=[(dnv, dnv.v)], inc=st)
                o_ps_free[op_] = dnv.v
        p.add("gpsimd", lambda e: e.wait_ge(st.h, st.v) and None)
        p.add("gpsimd", lambda e: e.wait_ge(ost.h, ost.v) and None)

        p.emit(nc, ctx)
    return nc


def build_l3():
    nc = bass.Bass("TRN2", target_bir_lowering=False)
    o_sl = nc.dram_tensor("o_sl", [DM, NT], BF16, kind="ExternalInput").ap()
    invd_sl = nc.dram_tensor("invd_sl", [16, NT], F32, kind="ExternalInput").ap()
    x_sl = nc.dram_tensor("x_sl", [DM, NT], F32, kind="ExternalInput").ap()
    wo_t = nc.dram_tensor("wo_t", [DM, DM], F32, kind="ExternalInput").ap()
    bo_pm = nc.dram_tensor("bo_pm", [128, 16], F32, kind="ExternalInput").ap()
    mod_all = nc.dram_tensor("mod_all", [128, 96], F32, kind="ExternalInput").ap()
    wdq_all = nc.dram_tensor("wdq_all", [DM, DM], BF16, kind="ExternalInput").ap()
    ident = nc.dram_tensor("ident", [128, 128], F32, kind="ExternalInput").ap()
    out_sl = nc.dram_tensor("out_sl", [DM, NT], F32, kind="ExternalOutput").ap()
    if L3_DEBUG:
        o_x1 = nc.dram_tensor("o_x1", [DM, NT], F32, kind="ExternalOutput").ap()
        o_h2q = nc.dram_tensor("o_h2q", [DM, NT], BF16, kind="ExternalOutput").ap()
        o_scl = nc.dram_tensor("o_scl", [8, DM], F32, kind="ExternalOutput").ap()
        o_g = nc.dram_tensor("o_g", [128, 32], F32, kind="ExternalOutput").ap()
    scr_r2 = nc.dram_tensor("scr_r2", [1, NT], F32, kind="Internal").ap()
    scr_m2 = nc.dram_tensor("scr_m2", [1, NT], F32, kind="Internal").ap()
    scr_rs = nc.dram_tensor("scr_rs", [8, DM], F32, kind="Internal").ap()
    scr_s2 = nc.dram_tensor("scr_s2", [8, DM], F32, kind="Internal").ap()

    p = Prog()
    ld = p.sem("ld")
    prp = p.sem("prp")
    ivb = p.sem("ivb")
    onr = p.sem("onr")
    wol = p.sem("wol")
    wob = p.sem("wob")
    omm = p.sem("omm")
    epa = p.sem("epa")
    epv = p.sem("epv")
    xld = p.sem("xld")
    sqd = p.sem("sqd")
    x1c = p.sem("x1c")
    smm = p.sem("smm")
    rwu = p.sem("rwu")
    rbc = p.sem("rbc")
    h2d = p.sem("h2d")
    trd = p.sem("trd")
    amx_s = p.sem("amx_s")
    qsu = p.sem("qsu")
    qbc = p.sem("qbc")
    qdd = p.sem("qdd")
    wfl = p.sem("wfl")
    fmm = p.sem("fmm")
    fea = p.sem("fea")
    fev = p.sem("fev")
    st = p.sem("st")

    with ExitStack() as ctx:
        def sbt(name, shape, dtype=F32):
            return ctx.enter_context(nc.sbuf_tensor(name, shape, dtype))
        o_sb = sbt("o_sb", [128, 16 * NT], BF16)
        x1 = sbt("x1", [128, 16 * NT])
        h2q = sbt("h2q", [128, 16 * NT], BF16)
        wo32 = [sbt(f"wo32_{i}", [128, 2 * NT]) for i in range(2)]
        wo_bf = [sbt(f"wo_bf{i}", [128, 2 * NT], BF16) for i in range(2)]
        wdqb = None
        wdqb = wo_bf
        inv_rep = [sbt(f"inv_rep{i}", [128, NT]) for i in range(2)]
        x32 = [sbt(f"x32_{i}", [128, NT]) for i in range(2)]
        ep = [sbt(f"ep{i}", [128, NT]) for i in range(2)]
        sqb = [sbt(f"sqb{i}", [128, NT], BF16) for i in range(2)]
        x1b = [sbt(f"x1b{i}", [128, NT], BF16) for i in range(2)]
        h2t = [sbt(f"h2t{i}", [128, NT]) for i in range(2)]
        th1 = x32
        rs_rep = [sbt(f"rs_rep{i}", [128, NT]) for i in range(2)]
        s2_rep = [wo32[0][:, 0:NT], wo32[1][:, 0:NT]]
        yq = ep
        q8 = [sbt(f"q8_{i}", [128, NT], FP8) for i in range(2)]
        out_sb = h2t
        rs2T = [sbt(f"rs2T{i}", [8, 128]) for i in range(2)]
        s2T = [sbt(f"s2T{i}", [8, 128]) for i in range(2)]
        amxk = sbt("amxk", [128, 8])
        rs2k = sbt("rs2k", [128, 8])
        rcpk = sbt("rcpk", [128, 8])
        mod_sb = sbt("mod_sb", [128, 96])
        ga_fm = sbt("ga_fm", [128, 16])
        gb_fm = sbt("gb_fm", [128, 16])
        gabo = sbt("gabo", [128, 16])
        cb1 = sbt("cb1", [128, 16])
        tsg = sbt("tsg", [128, 16])
        bo_sb = sbt("bo_sb", [128, 16])
        idt = sbt("idt", [128, 128])
        ones_bf = sbt("ones_bf", [128, 1], BF16)
        m2v = inv_rep[0][0:1, :]
        v2v = inv_rep[1][0:1, :]
        t2v = x32[0][0:1, :]
        r2x = x32[1][0:1, :]
        epsb = sbt("epsb", [1, 1])

        P0 = ctx.enter_context(nc.psum_tensor("P0", [128, NT], F32))
        P1 = ctx.enter_context(nc.psum_tensor("P1", [128, NT], F32))
        P2 = ctx.enter_context(nc.psum_tensor("P2", [128, NT], F32))
        P3 = ctx.enter_context(nc.psum_tensor("P3", [128, NT], F32))

        p.dma("sync", mod_sb[:], mod_all, inc=ld)
        p.dma("sync", bo_sb[:], bo_pm, inc=ld)
        p.dma("sync", idt[:], ident, inc=ld)
        p.dma("sync", o_sb.ap().rearrange("q (k n) -> q k n", k=16),
              o_sl.rearrange("(k q) n -> q k n", q=128), inc=ld)
        ld_all = ld.v
        p.add("vector", lambda e: e.memset(epsb[:], EPS), inc=prp)
        p.add("vector", lambda e: e.memset(ones_bf[:], 1.0), inc=prp)
        for (dst, col) in ((ga_fm, 32), (gb_fm, 80)):
            p.add("scalar",
                  (lambda dst=dst, col=col: lambda e: e.activation(
                      out=tsg[:], in_=mod_sb[:, col:col + 16], func=AF.Sign))(),
                  waits=[(ld, ld_all), (prp, prp.v)], inc=prp)
            p.add("scalar",
                  (lambda dst=dst, col=col: lambda e: e.activation(
                      out=dst[:], in_=mod_sb[:, col:col + 16], func=AF.Abs))(),
                  inc=prp)
            p.add("vector",
                  (lambda dst=dst: lambda e: e.tensor_scalar(
                      dst[:], dst[:], 0.25, None, op0=ALU.max))(),
                  waits=[(prp, prp.v)], inc=prp)
            p.add("gpsimd",
                  (lambda dst=dst: lambda e: e.tensor_mul(
                      dst[:], dst[:], tsg[:]))(),
                  waits=[(prp, prp.v)], inc=prp)
        p.add("gpsimd", lambda e: e.tensor_mul(gabo[:], ga_fm[:], bo_sb[:]),
              waits=[(prp, prp.v), (ld, ld_all)], inc=prp)
        p.add("vector", lambda e: e.tensor_scalar(
            cb1[:], mod_sb[:, 64:80], 1.0, None, op0=ALU.add),
            waits=[(ld, ld_all)], inc=prp)
        gates_done = prp.v

        iv_free = [0, 0]
        for K in range(16):
            ki = K % 2
            p.dma("gpsimd", inv_rep[ki][:],
                  invd_sl[K:K + 1, :].partition_broadcast(128),
                  waits=[(onr, iv_free[ki])], inc=ivb)
            p.add("vector",
                  (lambda K=K, ki=ki: lambda e: e.tensor_tensor(
                      o_sb[:, NT * K:NT * K + NT], o_sb[:, NT * K:NT * K + NT],
                      inv_rep[ki][:], op=ALU.mult))(),
                  waits=[(ivb, ivb.v), (ld, ld_all)], inc=onr)
            iv_free[ki] = onr.v
        onorm_done = onr.v
        wo_free = [0, 0]
        wob_free = [0, 0]
        ep_free = [0, 0]
        x32_free = [0, 0]
        ps_free = [0, 0]
        for of in range(16):
            oi = of % 2
            p.dma("sync", wo32[oi].ap().rearrange("q (k n) -> q k n", k=16),
                  wo_t.rearrange("(k q) n -> q k n", q=128)
                  [:, :, 128 * of:128 * of + 128],
                  waits=[(wob, wo_free[oi])], inc=wol)
            p.add("vector",
                  (lambda oi=oi: lambda e: e.tensor_copy(
                      wo_bf[oi][:], wo32[oi][:]))(),
                  waits=[(wol, wol.v), (omm, wob_free[oi])], inc=wob)
            p.dma("sync", x32[oi][:], x_sl[128 * of:128 * of + 128, :],
                  waits=[(epv, x32_free[oi])], inc=xld)
            ps = (P0, P1)[oi]
            for f in range(16):
                for half in range(2):
                    p.add("tensor",
                          (lambda ps=ps, oi=oi, f=f, half=half: lambda e:
                           e.matmul(
                              ps[:, 512 * half:512 * half + 512],
                              wo_bf[oi][:, 128 * f:128 * f + 128],
                              o_sb[:, NT * f + 512 * half:
                                   NT * f + 512 * half + 512],
                              start=(f == 0), stop=(f == 15)))(),
                          waits=([(wob, wob.v), (onr, onorm_done),
                                  (epa, ps_free[oi])]
                                 if f == 0 and half == 0 else []),
                          inc=omm)
            wob_free[oi] = omm.v
            p.add("scalar",
                  (lambda ps=ps, oi=oi, of=of: lambda e: e.activation(
                      out=ep[oi][:], in_=ps[:], func=AF.Identity,
                      scale=ga_fm[:, of:of + 1], bias=gabo[:, of:of + 1]))(),
                  waits=[(omm, omm.v), (prp, gates_done),
                         (epv, ep_free[oi])], inc=epa)
            ps_free[oi] = epa.v
            p.add("vector",
                  (lambda oi=oi, of=of: lambda e: e.tensor_add(
                      x1[:, NT * of:NT * of + NT], ep[oi][:], x32[oi][:]))(),
                  waits=[(epa, epa.v), (xld, xld.v)], inc=epv)
            ep_free[oi] = epv.v
            x32_free[oi] = epv.v
            wo_free[oi] = wob.v
        oproj_done = epv.v

        sq_free = [0, 0]
        x1b_free = [0, 0]
        for K in range(16):
            ki = K % 2
            p.add("scalar",
                  (lambda K=K, ki=ki: lambda e: e.activation(
                      out=sqb[ki][:], in_=x1[:, NT * K:NT * K + NT],
                      func=AF.Square))(),
                  waits=[(epv, oproj_done), (smm, sq_free[ki])], inc=sqd)
            p.add("vector",
                  (lambda K=K, ki=ki: lambda e: e.tensor_copy(
                      x1b[ki][:], x1[:, NT * K:NT * K + NT]))(),
                  waits=[(epv, oproj_done), (smm, x1b_free[ki])], inc=x1c)
            for half in range(2):
                p.add("tensor",
                      (lambda K=K, ki=ki, half=half: lambda e: e.matmul(
                          P2[0:1, 512 * half:512 * half + 512],
                          ones_bf[:],
                          x1b[ki][:, 512 * half:512 * half + 512],
                          start=(K == 0), stop=(K == 15)))(),
                      waits=[(x1c, x1c.v), (prp, 2)] if half == 0 else [],
                      inc=smm)
                p.add("tensor",
                      (lambda K=K, ki=ki, half=half: lambda e: e.matmul(
                          P3[0:1, 512 * half:512 * half + 512],
                          ones_bf[:],
                          sqb[ki][:, 512 * half:512 * half + 512],
                          start=(K == 0), stop=(K == 15)))(),
                      waits=[(sqd, sqd.v)] if half == 0 else [], inc=smm)
            sq_free[ki] = smm.v
            x1b_free[ki] = smm.v
        smm_all = smm.v
        p.add("vector", lambda e: e.tensor_scalar(
            m2v, P2[0:1, :], 1.0 / DM, None, op0=ALU.mult),
            waits=[(smm, smm_all)], inc=prp)
        p.add("vector", lambda e: e.tensor_scalar(
            v2v, P3[0:1, :], 1.0 / DM, None, op0=ALU.mult), inc=prp)
        p.add("vector", lambda e: e.tensor_mul(t2v, m2v, m2v), inc=prp)
        p.add("vector", lambda e: e.tensor_sub(v2v, v2v, t2v), inc=prp)
        p.add("scalar", lambda e: e.activation(
            out=v2v, in_=v2v, func=AF.Ln, bias=epsb[:]),
            waits=[(prp, prp.v)], inc=prp)
        p.add("vector", lambda e: e.tensor_scalar(
            m2v, m2v, -1.0, None, op0=ALU.mult), inc=prp)
        p.add("scalar", lambda e: e.activation(
            out=r2x, in_=v2v, func=AF.Exp, scale=-0.5),
            waits=[(prp, prp.v)], inc=prp)
        p.add("gpsimd", lambda e: e.tensor_mul(m2v, m2v, r2x),
            waits=[(prp, prp.v)], inc=prp)
        rows2_done = prp.v
        p.dma("gpsimd", scr_r2, r2x, waits=[(prp, rows2_done)], inc=rwu)
        p.dma("gpsimd", scr_m2, m2v, inc=rwu)
        rwu_all = rwu.v
        p.dma("gpsimd", rs_rep[0][:], scr_r2.partition_broadcast(128),
              waits=[(rwu, rwu_all)], inc=rbc)
        p.dma("gpsimd", rs_rep[1][:], scr_m2.partition_broadcast(128), inc=rbc)
        r2bc_done = rbc.v
        r2_rep = rs_rep[0]
        m2n_rep = rs_rep[1]

        h2_free = [0, 0]
        th_free = [0, 0]
        qrep_free = [0, 0]
        for K in range(16):
            ki = K % 2
            p.add("vector",
                  (lambda K=K, ki=ki: lambda e: e.tensor_tensor(
                      th1[ki][:], x1[:, NT * K:NT * K + NT], r2_rep[:],
                      op=ALU.mult))(),
                  waits=[(rbc, r2bc_done), (h2d, th_free[ki])], inc=h2d)
            p.add("vector",
                  (lambda ki=ki: lambda e: e.scalar_tensor_tensor(
                      th1[ki][:], in0=m2n_rep[:], scalar=1.0, in1=th1[ki][:],
                      op0=ALU.mult, op1=ALU.add))(),
                  inc=h2d)
            p.add("scalar",
                  (lambda K=K, ki=ki: lambda e: e.activation(
                      out=h2t[ki][:], in_=th1[ki][:], func=AF.Identity,
                      scale=cb1[:, K:K + 1],
                      bias=mod_sb[:, 48 + K:48 + K + 1]))(),
                  waits=[(h2d, h2d.v), (qdd, h2_free[ki])], inc=h2d)
            th_free[ki] = h2d.v
            amx_prev = [amx_s.v, amx_s.v]
            for j in range(8):
                ji = j % 2
                psl = P2[:, 512 * ji:512 * ji + 128]
                p.add("tensor",
                      (lambda ki=ki, j=j, psl=psl: lambda e: e.transpose(
                          psl, h2t[ki][:, 128 * j:128 * j + 128], idt[:]))(),
                      waits=[(h2d, h2d.v), (ld, ld_all),
                             (amx_s, amx_prev[ji])],
                      inc=trd)
                p.add("vector",
                      (lambda j=j, psl=psl: lambda e: e.tensor_reduce(
                          amxk[:, j:j + 1], psl, axis=AX.X,
                          op=ALU.max, apply_absolute_value=True))(),
                      waits=[(trd, trd.v)], inc=amx_s)
                amx_prev[ji] = amx_s.v
            p.add("vector", lambda e: e.tensor_scalar(
                rs2k[:], amxk[:], 1.0 / 224.0, None, op0=ALU.mult),
                waits=[(amx_s, amx_s.v)], inc=amx_s)
            p.add("scalar", lambda e: e.activation(
                out=amxk[:], in_=rs2k[:], func=AF.Ln),
                waits=[(amx_s, amx_s.v)], inc=amx_s)
            p.add("scalar", lambda e: e.activation(
                out=rcpk[:], in_=amxk[:], func=AF.Exp, scale=-1.0),
                inc=amx_s)
            p.add("tensor",
                  (lambda: lambda e: e.transpose(
                      P2[0:8, 0:128], rcpk[:, 0:8], idt[:]))(),
                  waits=[(amx_s, amx_s.v)], inc=trd)
            p.add("tensor",
                  (lambda: lambda e: e.transpose(
                      P2[0:8, 512:640], rs2k[:, 0:8], idt[:]))(),
                  inc=trd)
            p.add("vector",
                  (lambda ki=ki: lambda e: e.tensor_copy(
                      rs2T[ki][:], P2[0:8, 0:128]))(),
                  waits=[(trd, trd.v), (qsu, qsu.v)], inc=amx_s)
            p.add("vector",
                  (lambda ki=ki: lambda e: e.tensor_copy(
                      s2T[ki][:], P2[0:8, 512:640]))(),
                  inc=amx_s)
            p.dma("gpsimd", scr_rs[:, 128 * K:128 * K + 128],
                  rs2T[ki][:], waits=[(amx_s, amx_s.v)], inc=qsu)
            p.dma("gpsimd", scr_s2[:, 128 * K:128 * K + 128],
                  s2T[ki][:], inc=qsu)
            qsu_here = qsu.v
            p.dma("sync", yq[ki][:],
                  scr_rs[:, 128 * K:128 * K + 128].partition_broadcast(128),
                  waits=[(qsu, qsu_here), (qdd, qrep_free[ki])], inc=qbc)
            p.dma("sync", s2_rep[ki],
                  scr_s2[:, 128 * K:128 * K + 128].partition_broadcast(128),
                  inc=qbc)
            qbc_here = qbc.v
            p.add("vector",
                  (lambda ki=ki: lambda e: e.tensor_tensor(
                      yq[ki][:], h2t[ki][:], yq[ki][:], op=ALU.mult))(),
                  waits=[(qbc, qbc_here)], inc=qdd)
            p.add("vector",
                  (lambda ki=ki: lambda e: e.tensor_copy(q8[ki][:], yq[ki][:]))(),
                  inc=qdd)
            p.add("vector",
                  (lambda K=K, ki=ki: lambda e: e.tensor_tensor(
                      h2q[:, NT * K:NT * K + NT], q8[ki][:], s2_rep[ki],
                      op=ALU.mult))(),
                  inc=qdd)
            h2_free[ki] = qdd.v
            qrep_free[ki] = qdd.v
        quant_done = qdd.v
        if L3_DEBUG:
            p.dma("gpsimd", o_g[:, 0:16], ga_fm.ap(), waits=[(prp, gates_done)],
                  inc=st)
            p.dma("gpsimd", o_g[:, 16:32], gb_fm.ap(), inc=st)
            p.dma("gpsimd", o_x1.rearrange("(k q) n -> q k n", q=128),
                  x1.ap().rearrange("q (k n) -> q k n", k=16),
                  waits=[(epv, oproj_done)], inc=st)
            p.dma("gpsimd", o_h2q.rearrange("(k q) n -> q k n", q=128),
                  h2q.ap().rearrange("q (k n) -> q k n", k=16),
                  waits=[(qdd, quant_done)], inc=st)
            p.dma("gpsimd", o_scl, scr_rs, waits=[(qsu, qsu.v)], inc=st)

        wf_free = [0, 0]
        fps_free = [0, 0]
        fep_free = [0, 0]
        fout_free = [0, 0]
        for of in range(16):
            oi = of % 2
            p.dma("sync", wdqb[oi].ap().rearrange("q (k n) -> q k n", k=16),
                  wdq_all.rearrange("(k q) n -> q k n", q=128)
                  [:, :, 128 * of:128 * of + 128],
                  waits=[(fmm, wf_free[oi])], inc=wfl)
            wfl_here = wfl.v
            ps = (P0, P1)[oi]
            for f in range(16):
                for half in range(2):
                    p.add("tensor",
                          (lambda ps=ps, oi=oi, f=f, half=half: lambda e:
                           e.matmul(
                              ps[:, 512 * half:512 * half + 512],
                              wdqb[oi][:, 128 * f:128 * f + 128],
                              h2q[:, NT * f + 512 * half:
                                   NT * f + 512 * half + 512],
                              start=(f == 0), stop=(f == 15)))(),
                          waits=([(wfl, wfl_here), (qdd, quant_done),
                                  (fea, fps_free[oi])]
                                 if f == 0 and half == 0 else []),
                          inc=fmm)
            wf_free[oi] = fmm.v
            p.add("scalar",
                  (lambda ps=ps, oi=oi, of=of: lambda e: e.activation(
                      out=ep[oi][:], in_=ps[:], func=AF.Identity,
                      scale=gb_fm[:, of:of + 1]))(),
                  waits=[(fmm, fmm.v), (fev, fep_free[oi])], inc=fea)
            fps_free[oi] = fea.v
            p.add("vector",
                  (lambda oi=oi, of=of: lambda e: e.tensor_add(
                      out_sb[oi][:], ep[oi][:],
                      x1[:, NT * of:NT * of + NT]))(),
                  waits=[(fea, fea.v), (st, fout_free[oi])], inc=fev)
            fep_free[oi] = fev.v
            p.dma("gpsimd", out_sl[128 * of:128 * of + 128, :],
                  out_sb[oi].ap(), waits=[(fev, fev.v)], inc=st)
            fout_free[oi] = st.v
        p.add("gpsimd", lambda e: e.wait_ge(st.h, st.v) and None)

        p.emit(nc, ctx)
    return nc


def _bf(x):
    return np.ascontiguousarray(x, dtype=ml_dtypes.bfloat16)


def _f32(x):
    return np.ascontiguousarray(x, dtype=np.float32)


_NC_CACHE = {}


def _get_nc(name):
    if name not in _NC_CACHE:
        _NC_CACHE[name] = {"l1": build_l1, "l2": build_l2, "l3": build_l3}[name]()
    return _NC_CACHE[name]


def kernel(**inputs):
    x = np.asarray(inputs["L0_attn_hidden_states"])[0]
    x_fm = np.ascontiguousarray(x.T)
    cvec = np.ascontiguousarray(np.asarray(inputs["L0_c"]))
    wmod = np.asarray(inputs["L0_wmod"])
    wffn = np.asarray(inputs["L0_ffn_weight"])
    wffn_t = np.ascontiguousarray(wffn.T)
    Wq_t = np.ascontiguousarray(np.asarray(inputs["L0_attn_Wq"]).T)
    Wk_t = np.ascontiguousarray(np.asarray(inputs["L0_attn_Wk"]).T)
    Wv_t = np.ascontiguousarray(np.asarray(inputs["L0_attn_Wv"]).T)
    Wo_t = np.ascontiguousarray(np.asarray(inputs["L0_attn_Wo"]).T)
    bq = np.asarray(inputs["L0_attn_bq"])
    bk = np.asarray(inputs["L0_attn_bk"])
    bv = np.asarray(inputs["L0_attn_bv"])
    bo = np.asarray(inputs["L0_attn_bo"])
    ident = np.eye(128, dtype=np.float32)
    cores = list(range(NCORE))

    im1 = []
    for c in cores:
        im1.append({
            "x_sl": np.ascontiguousarray(x_fm[:, NT * c:NT * (c + 1)]),
            "c_in": cvec,
            "wmod_sl": np.ascontiguousarray(wmod[:, 1536 * c:1536 * (c + 1)]),
            "wfn": np.ascontiguousarray(wffn[256 * c:256 * (c + 1), :]),
            "wft": np.ascontiguousarray(wffn_t[:, 256 * c:256 * (c + 1)]),
            "ident": ident,
        })
    r1 = run_bass_kernel_spmd(_get_nc("l1"), im1, core_ids=cores).results

    mod_all = np.concatenate([r1[c]["o_mod"] for c in cores], axis=1)
    m8 = np.stack([r1[c]["o_stats"][0] for c in cores], axis=0)
    v8 = np.stack([r1[c]["o_stats"][1] for c in cores], axis=0)
    wdq_all = np.concatenate([r1[c]["o_wdq"] for c in cores], axis=1)

    jidx = np.arange(128)
    tri = np.where(jidx[:, None] < jidx[None, :], np.float32(NEG),
                   np.float32(0)).astype(np.float32)
    tris = np.where((jidx[:, None] >= SINKS) & (jidx[:, None] < jidx[None, :]),
                    np.float32(NEG), np.float32(0)).astype(np.float32)
    bc = np.full((128, 1), -CSH, dtype=np.float32)
    bcs = bc.copy()
    bcs[SINKS:, 0] = -1.0e38
    im2 = []
    for c in cores:
        sl = slice(256 * c, 256 * c + 256)
        b_pm = np.stack([bq[sl].reshape(2, 128).T, bk[sl].reshape(2, 128).T,
                         bv[sl].reshape(2, 128).T], axis=1).reshape(128, 6)
        im2.append({
            "x_fm": x_fm,
            "wq_t": np.ascontiguousarray(Wq_t[:, sl]),
            "wk_t": np.ascontiguousarray(Wk_t[:, sl]),
            "wv_t": np.ascontiguousarray(Wv_t[:, sl]),
            "b_pm": np.ascontiguousarray(b_pm, dtype=np.float32),
            "mod_all": mod_all, "m8": m8, "v8": v8,
            "tri_in": tri, "tris_in": tris, "bc_in": bc, "bcs_in": bcs,
        })
    r2 = run_bass_kernel_spmd(_get_nc("l2"), im2, core_ids=cores).results

    o_all = np.concatenate([r2[c]["o_raw"] for c in cores], axis=0)
    invd_all = np.concatenate([r2[c]["o_invd"] for c in cores], axis=0)

    bo_pm = np.ascontiguousarray(bo.reshape(16, 128).T, dtype=np.float32)
    im3 = []
    for c in cores:
        tsl = slice(NT * c, NT * (c + 1))
        im3.append({
            "o_sl": np.ascontiguousarray(o_all[:, tsl]),
            "invd_sl": np.ascontiguousarray(invd_all[:, tsl]),
            "x_sl": np.ascontiguousarray(x_fm[:, tsl]),
            "wo_t": Wo_t,
            "bo_pm": bo_pm,
            "mod_all": mod_all,
            "wdq_all": wdq_all,
            "ident": ident,
        })
    r3 = run_bass_kernel_spmd(_get_nc("l3"), im3, core_ids=cores).results
    out_fm = np.concatenate([r3[c]["out_sl"] for c in cores], axis=1)
    return np.ascontiguousarray(out_fm.T)[None, :, :]


# version 03
# speedup vs baseline: 1.3425x; 1.3425x over previous
import math
import numpy as np
import ml_dtypes
from contextlib import ExitStack

import concourse.bass as bass
import concourse.mybir as mybir
from concourse.bass_utils import run_bass_kernel_spmd

F32 = mybir.dt.float32
BF16 = mybir.dt.bfloat16
FP8 = mybir.dt.float8e4
AF = mybir.ActivationFunctionType
ALU = mybir.AluOpType
AX = mybir.AxisListType

NQ = 8192
DM = 2048
NT = 1024
NH = 16
HD = 128
WIN = 512
SINKS = 4
NCORE = 8
CSH = 20.0
ASCALE = 1.0 / math.sqrt(HD)
NEG = -1.0e30
EPS = 1e-5
L3_DEBUG = False
ENGS = ("sync", "vector", "scalar", "tensor", "gpsimd")


class Sem:
    def __init__(self, name):
        self.name = name
        self.v = 0
        self.h = None

    def bump(self, n):
        self.v += n
        return self.v


class Prog:

    def __init__(self):
        self.ops = {e: [] for e in ENGS}
        self.sems = []

    def sem(self, name):
        s = Sem(name)
        self.sems.append(s)
        return s

    def add(self, eng, fn, waits=(), inc=None, inc_n=1):
        planned = None
        if inc is not None:
            planned = inc.bump(inc_n)
        self.ops[eng].append((fn, list(waits), inc, inc_n))
        return planned

    def dma(self, eng, out, in_, waits=(), inc=None, **kw):
        return self.add(eng, lambda e: e.dma_start(out=out, in_=in_, **kw),
                        waits=waits, inc=inc, inc_n=16)

    def emit(self, nc, ctx):
        for s in self.sems:
            s.h = ctx.enter_context(nc.semaphore(s.name))
        block = ctx.enter_context(nc.Block())

        def mk(eng_name):
            def body(eng):
                for fn, waits, inc, inc_n in self.ops[eng_name]:
                    for (s, v) in waits:
                        eng.wait_ge(s.h, v)
                    instr = fn(eng)
                    if inc is not None:
                        assert instr is not None
                        instr.then_inc(inc.h, inc_n)
            return body

        for name in ENGS:
            if not self.ops[name]:
                continue
            getattr(block, name)(mk(name))


def build_l1():
    nc = bass.Bass("TRN2", target_bir_lowering=False)
    x_sl = nc.dram_tensor("x_sl", [DM, NT], F32, kind="ExternalInput").ap()
    c_in = nc.dram_tensor("c_in", [1, DM], F32, kind="ExternalInput").ap()
    wmod_sl = nc.dram_tensor("wmod_sl", [DM, 1536], F32, kind="ExternalInput").ap()
    wfn = nc.dram_tensor("wfn", [256, DM], F32, kind="ExternalInput").ap()
    wft = nc.dram_tensor("wft", [DM, 256], F32, kind="ExternalInput").ap()
    ident = nc.dram_tensor("ident", [128, 128], F32, kind="ExternalInput").ap()
    o_stats = nc.dram_tensor("o_stats", [2, NT], F32, kind="ExternalOutput").ap()
    o_mod = nc.dram_tensor("o_mod", [128, 12], F32, kind="ExternalOutput").ap()
    o_wdq = nc.dram_tensor("o_wdq", [DM, 256], BF16, kind="ExternalOutput").ap()
    scr_rs = nc.dram_tensor("scr_rs", [16, 256], F32, kind="Internal").ap()
    scr_s2 = nc.dram_tensor("scr_s2", [16, 256], F32, kind="Internal").ap()

    p = Prog()
    ld = p.sem("ld")
    sil = p.sem("sil")
    wld = p.sem("wld")
    xld = p.sem("xld")
    cstv = p.sem("cstv")
    csts = p.sem("csts")
    smm = p.sem("smm")
    mmm = p.sem("mmm")
    ved = p.sem("ved")
    sca = p.sem("sca")
    bcd = p.sem("bcd")
    qd = p.sem("qd")
    st = p.sem("st")

    with ExitStack() as ctx:
        def sbt(name, shape, dtype=F32):
            return ctx.enter_context(nc.sbuf_tensor(name, shape, dtype))
        cfm = sbt("cfm", [128, 16])
        scfm = sbt("scfm", [128, 16])
        wmodb = [sbt(f"wmodb{i}", [128, 1536]) for i in range(2)]
        x32 = [sbt(f"x32_{i}", [128, NT]) for i in range(2)]
        xbb = [sbt(f"xbb{i}", [128, NT], BF16) for i in range(2)]
        sqb = [sbt(f"sqb{i}", [128, NT], BF16) for i in range(2)]
        wfns = sbt("wfns", [128, 2 * DM])
        wfts = sbt("wfts", [128, 16 * 256])
        idt = sbt("idt", [128, 128])
        modsb = sbt("modsb", [128, 12])
        ones_bf = sbt("ones_bf", [128, 1], BF16)
        t_mean = sbt("t_mean", [1, NT])
        t_var = sbt("t_var", [1, NT])
        t_m2 = sbt("t_m2", [1, NT])
        amx = sbt("amx", [128, 32])
        sT = sbt("sT", [16, 256])
        rs2T = sbt("rs2T", [16, 256])
        s2T = sbt("s2T", [16, 256])
        repA = [sbt(f"repA{i}", [128, 256]) for i in range(2)]
        repB = [sbt(f"repB{i}", [128, 256]) for i in range(2)]
        yq = [sbt(f"yq{i}", [128, 256]) for i in range(2)]
        q8 = [sbt(f"q8_{i}", [128, 256], FP8) for i in range(2)]
        wdq = sbt("wdq", [128, 16 * 256], BF16)
        epsb = sbt("epsb", [1, 1])

        psmod = ctx.enter_context(nc.psum_tensor("psmod", [128, 12], F32))
        ps_sa = ctx.enter_context(nc.psum_tensor("ps_sa", [1, 512], F32))
        ps_sb = ctx.enter_context(nc.psum_tensor("ps_sb", [1, 512], F32))
        ps_qa = ctx.enter_context(nc.psum_tensor("ps_qa", [1, 512], F32))
        ps_qb = ctx.enter_context(nc.psum_tensor("ps_qb", [1, 512], F32))
        ps_t = ctx.enter_context(nc.psum_tensor("ps_t", [16, 128], F32))

        p.dma("sync", cfm[:], c_in.rearrange("o (k q) -> q (o k)", q=128), inc=ld,
              allow_slow_non_contiguous=True)
        p.dma("sync", idt[:], ident, inc=ld)
        p.dma("sync", wfns.ap().rearrange("q (a n) -> q a n", a=2),
              wfn.rearrange("(a q) n -> q a n", q=128), inc=ld)
        p.dma("sync", wfts.ap().rearrange("q (k n) -> q k n", k=16),
              wft.rearrange("(k q) n -> q k n", q=128), inc=ld)
        ld_all = ld.v

        p.add("scalar", lambda e: e.activation(out=scfm[:], in_=cfm[:], func=AF.Silu),
              waits=[(ld, ld_all)], inc=sil)

        for K in range(16):
            p.dma("sync", wmodb[K % 2][:], wmod_sl[128 * K:128 * K + 128, :],
                  waits=([(mmm, (K - 2) * 12 + 12)] if K >= 2 else []), inc=wld)
            for oc in range(12):
                p.add("tensor",
                      (lambda K=K, oc=oc: lambda e: e.matmul(
                          psmod[:, oc:oc + 1],
                          wmodb[K % 2][:, 128 * oc:128 * oc + 128],
                          scfm[:, K:K + 1],
                          start=(K == 0 and oc == 0),
                          stop=(K == 15 and oc == 11)))(),
                      waits=([(wld, 16 * (K + 1)), (sil, 1)] if oc == 0 else []),
                      inc=mmm)
        p.add("vector", lambda e: e.tensor_copy(modsb[:], psmod[:]),
              waits=[(mmm, mmm.v)], inc=ved)
        p.dma("gpsimd", o_mod, modsb[:], waits=[(ved, ved.v)], inc=st)

        p.add("vector", lambda e: e.memset(ones_bf[:], 1.0), inc=ved)
        ones_ready = ved.v
        for K in range(16):
            w = [(cstv, K - 1), (csts, K - 1)] if K >= 2 else []
            p.dma("sync", x32[K % 2][:], x_sl[128 * K:128 * K + 128, :],
                  waits=w, inc=xld)
            wr = [(smm, 4 * (K - 1))] if K >= 2 else []
            p.add("vector",
                  (lambda K=K: lambda e: e.tensor_copy(xbb[K % 2][:], x32[K % 2][:]))(),
                  waits=[(xld, 16 * (K + 1))] + wr, inc=cstv)
            p.add("scalar",
                  (lambda K=K: lambda e: e.activation(
                      out=sqb[K % 2][:], in_=x32[K % 2][:], func=AF.Square))(),
                  waits=[(xld, 16 * (K + 1))] + wr, inc=csts)
            for h in range(2):
                p.add("tensor",
                      (lambda K=K, h=h: lambda e: e.matmul(
                          (ps_sa if h == 0 else ps_sb)[:],
                          ones_bf[:], xbb[K % 2][:, 512 * h:512 * h + 512],
                          start=(K == 0), stop=(K == 15)))(),
                      waits=[(cstv, K + 1), (ved, ones_ready)], inc=smm)
                p.add("tensor",
                      (lambda K=K, h=h: lambda e: e.matmul(
                          (ps_qa if h == 0 else ps_qb)[:],
                          ones_bf[:], sqb[K % 2][:, 512 * h:512 * h + 512],
                          start=(K == 0), stop=(K == 15)))(),
                      waits=[(csts, K + 1)], inc=smm)
        smm_all = smm.v
        for h, (pss, psq) in enumerate(((ps_sa, ps_qa), (ps_sb, ps_qb))):
            sl = slice(512 * h, 512 * h + 512)
            p.add("vector",
                  (lambda pss=pss, sl=sl: lambda e: e.tensor_scalar(
                      t_mean[0:1, sl], pss[:], 1.0 / DM, None, op0=ALU.mult))(),
                  waits=[(smm, smm_all)], inc=ved)
            p.add("vector",
                  (lambda psq=psq, sl=sl: lambda e: e.tensor_scalar(
                      t_var[0:1, sl], psq[:], 1.0 / DM, None, op0=ALU.mult))(),
                  inc=ved)
            p.add("gpsimd",
                  (lambda sl=sl: lambda e: e.tensor_mul(
                      t_m2[0:1, sl], t_mean[0:1, sl], t_mean[0:1, sl]))(),
                  waits=[(ved, ved.v)], inc=ved)
            p.add("gpsimd",
                  (lambda sl=sl: lambda e: e.tensor_sub(
                      t_var[0:1, sl], t_var[0:1, sl], t_m2[0:1, sl]))(),
                  inc=ved)
        p.dma("gpsimd", o_stats[0:1, :], t_mean[:], waits=[(ved, ved.v)], inc=st)
        p.dma("gpsimd", o_stats[1:2, :], t_var[:], inc=st)

        for i in range(2):
            p.add("vector",
                  (lambda i=i: lambda e: e.tensor_reduce(
                      amx[:, 16 * i:16 * i + 16],
                      wfns.ap().rearrange("q (a k f) -> q (a k) f", a=2, k=16)
                      [:, 16 * i:16 * i + 16, :],
                      axis=AX.X, op=ALU.max, apply_absolute_value=True))(),
                  waits=[(ld, ld_all)], inc=ved)
        amx_done = ved.v
        for i in range(2):
            p.add("tensor",
                  (lambda i=i: lambda e: e.transpose(
                      ps_t[:], amx[:, 16 * i:16 * i + 16], idt[:]))(),
                  waits=[(ved, amx_done), (ld, ld_all)] + ([(ved, ved.v)] if i else []),
                  inc=mmm)
            p.add("vector",
                  (lambda i=i: lambda e: e.tensor_scalar(
                      sT[:, 128 * i:128 * i + 128], ps_t[:], 1.0 / 448.0, None,
                      op0=ALU.mult))(),
                  waits=[(mmm, mmm.v)], inc=ved)
        p.add("vector", lambda e: e.tensor_scalar(s2T[:], sT[:], 2.0, None,
                                                  op0=ALU.mult), inc=ved)
        p.add("scalar", lambda e: e.activation(out=sT[:], in_=s2T[:],
                                               func=AF.Ln),
              waits=[(ved, ved.v)], inc=ved)
        p.add("scalar", lambda e: e.activation(out=rs2T[:], in_=sT[:],
                                               func=AF.Exp, scale=-1.0),
              inc=ved)
        sc_done = ved.v
        p.dma("gpsimd", scr_rs, rs2T[:], waits=[(ved, sc_done)], inc=sca)
        p.dma("gpsimd", scr_s2, s2T[:], inc=sca)
        sca_all = sca.v

        for K in range(16):
            i = K % 2
            w = [(sca, sca_all)] + ([(qd, (K - 2) + 1)] if K >= 2 else [])
            p.dma("sync", repA[i][:], scr_rs[K:K + 1, :].partition_broadcast(128),
                  waits=w, inc=bcd)
            p.dma("sync", repB[i][:], scr_s2[K:K + 1, :].partition_broadcast(128),
                  inc=bcd)
            bc_here = bcd.v
            p.add("vector",
                  (lambda K=K, i=i: lambda e: e.tensor_mul(
                      yq[i][:], wfts[:, 256 * K:256 * K + 256], repA[i][:]))(),
                  waits=[(bcd, bc_here), (ld, ld_all)], inc=ved)
            p.add("gpsimd",
                  (lambda i=i: lambda e: e.tensor_copy(q8[i][:], yq[i][:]))(),
                  waits=[(ved, ved.v)], inc=ved)
            p.add("vector",
                  (lambda K=K, i=i: lambda e: e.tensor_mul(
                      wdq[:, 256 * K:256 * K + 256], q8[i][:], repB[i][:]))(),
                  waits=[(ved, ved.v)], inc=qd)
        p.dma("gpsimd", o_wdq.rearrange("(k q) n -> q k n", q=128),
              wdq.ap().rearrange("q (k n) -> q k n", k=16),
              waits=[(qd, qd.v)], inc=st)
        st_all = st.v
        p.add("gpsimd", lambda e: None if e.wait_ge(st.h, st_all) else None)

        p.emit(nc, ctx)
    return nc


def build_l2():
    nc = bass.Bass("TRN2", target_bir_lowering=False)
    x_fm = nc.dram_tensor("x_fm", [DM, NQ], F32, kind="ExternalInput").ap()
    wq_t = nc.dram_tensor("wq_t", [DM, 256], F32, kind="ExternalInput").ap()
    wk_t = nc.dram_tensor("wk_t", [DM, 256], F32, kind="ExternalInput").ap()
    wv_t = nc.dram_tensor("wv_t", [DM, 256], F32, kind="ExternalInput").ap()
    b_pm = nc.dram_tensor("b_pm", [128, 6], F32, kind="ExternalInput").ap()
    mod_all = nc.dram_tensor("mod_all", [128, 96], F32, kind="ExternalInput").ap()
    m8 = nc.dram_tensor("m8", [8, NT], F32, kind="ExternalInput").ap()
    v8 = nc.dram_tensor("v8", [8, NT], F32, kind="ExternalInput").ap()
    tri_in = nc.dram_tensor("tri_in", [128, 128], F32, kind="ExternalInput").ap()
    tris_in = nc.dram_tensor("tris_in", [128, 128], F32, kind="ExternalInput").ap()
    bc_in = nc.dram_tensor("bc_in", [128, 1], F32, kind="ExternalInput").ap()
    bcs_in = nc.dram_tensor("bcs_in", [128, 1], F32, kind="ExternalInput").ap()
    o_raw = nc.dram_tensor("o_raw", [256, NQ], BF16, kind="ExternalOutput").ap()
    o_invd = nc.dram_tensor("o_invd", [2, NQ], F32, kind="ExternalOutput").ap()
    scr_r = nc.dram_tensor("scr_r", [8, NT], F32, kind="Internal").ap()
    scr_mrn = nc.dram_tensor("scr_mrn", [8, NT], F32, kind="Internal").ap()
    ws_d = [nc.dram_tensor(f"ws_d{m}", [16, 128, 256], BF16, kind="Internal").ap()
            for m in range(3)]

    p = Prog()
    ld = p.sem("ld")
    prp = p.sem("prp")
    wl = p.sem("wl")
    wsd = p.sem("wsd")
    uvm = p.sem("uvm")
    rup = p.sem("rup")
    bcd = p.sem("bcd")
    xl = p.sem("xl")
    csd = p.sem("csd")
    wgl = p.sem("wgl")
    qmm = p.sem("qmm")
    epd = p.sem("epd")
    epa = p.sem("epa")
    vtd = p.sem("vtd")
    smm = p.sem("smm")
    trid = p.sem("trid")
    exd = p.sem("exd")
    pvm = p.sem("pvm")
    dnv = p.sem("dnv")
    oev = p.sem("oev")
    ost = p.sem("ost")
    dnm = p.sem("dnm")
    st = p.sem("st")

    with ExitStack() as ctx:
        def sbt(name, shape, dtype=F32):
            return ctx.enter_context(nc.sbuf_tensor(name, shape, dtype))
        q_sb = [sbt(f"q_sb{h}", [128, NQ], BF16) for h in range(2)]
        k_sb = [sbt(f"k_sb{h}", [128, NQ], BF16) for h in range(2)]
        v_tm = [sbt(f"v_tm{h}", [128, NQ], BF16) for h in range(2)]
        ones_f32 = sbt("ones_f32", [128, 1])
        dd_c = sbt("dd_c", [1, NT])
        invd_c = sbt("invd_c", [1, NT])
        xb = [sbt(f"xb{k}", [128, NT], BF16) for k in range(16)]
        x32 = [sbt(f"x32_{i}", [128, NT]) for i in range(2)]
        r_rep = [sbt(f"r_rep{i}", [128, NT]) for i in range(2)]
        mrn_rep = [sbt(f"mrn_rep{i}", [128, NT]) for i in range(2)]
        wgrp = [sbt(f"wgrp{i}", [128, 16 * 128], BF16) for i in range(2)]
        w32 = [sbt(f"w32_{i}", [128, 256]) for i in range(2)]
        wsc = [sbt(f"wsc{i}", [128, 256], BF16) for i in range(2)]
        ep_t = [sbt(f"ep_t{i}", [128, NT]) for i in range(2)]
        vtmp = [sbt(f"vtmp{i}", [128, NT], BF16) for i in range(2)]
        ost_b = [sbt(f"ost_b{i}", [128, NT], BF16) for i in range(2)]
        p_bf = [sbt(f"p_bf{i}", [128, NT], BF16) for i in range(2)]
        pair = sbt("pair", [128, NT], BF16)
        acc = sbt("acc", [128, NT])
        mod_sb = sbt("mod_sb", [128, 96])
        onepca = sbt("onepca", [128, 16])
        one_t = sbt("one_t", [128, 1])
        b_sb = sbt("b_sb", [128, 6])
        u_sb = sbt("u_sb", [128, 6])
        v_sb = sbt("v_sb", [128, 6])
        tri = sbt("tri", [128, 128])
        tris = sbt("tris", [128, 128])
        biasC = sbt("biasC", [128, 1])
        biasCs = sbt("biasCs", [128, 1])
        m8s = sbt("m8s", [8, 512])
        v8s = sbt("v8s", [8, 512])
        r8x = sbt("r8x", [8, 512])
        epsb = sbt("epsb", [8, 1])
        ones_bf = sbt("ones_bf", [128, 1], BF16)

        S0 = ctx.enter_context(nc.psum_tensor("S0", [128, NT], F32))
        S1 = ctx.enter_context(nc.psum_tensor("S1", [128, NT], F32))
        O0 = ctx.enter_context(nc.psum_tensor("O0", [128, NT], F32))
        O1 = ctx.enter_context(nc.psum_tensor("O1", [128, NT], F32))
        SP = [S0, S1]
        OP = [O0, O1]

        p.dma("sync", mod_sb[:], mod_all, inc=ld)
        p.dma("sync", b_sb[:], b_pm, inc=ld)
        p.dma("sync", tri[:], tri_in, inc=ld)
        p.dma("sync", tris[:], tris_in, inc=ld)
        p.dma("sync", biasC[:], bc_in, inc=ld)
        p.dma("sync", biasCs[:], bcs_in, inc=ld)
        ld_all = ld.v
        p.add("vector", lambda e: e.memset(ones_bf[:], 1.0), inc=prp)
        p.add("vector", lambda e: e.memset(ones_f32[:], 1.0), inc=prp)
        p.add("vector", lambda e: e.memset(epsb[:], EPS), inc=prp)
        p.add("vector", lambda e: e.memset(ones_bf[:], 1.0), inc=prp)
        p.add("vector", lambda e: e.memset(one_t[:], 1.0), inc=prp)
        p.add("scalar", lambda e: e.activation(
            out=onepca[:], in_=mod_sb[:, 16:32], func=AF.Identity,
            bias=one_t[:]),
            waits=[(ld, ld_all), (prp, prp.v)], inc=prp)
        onepca_ready = prp.v
        for hh in range(2):
            hsl = slice(512 * hh, 512 * hh + 512)
            p.dma("sync", m8s[:], m8[:, hsl], waits=[(rup, rup.v)], inc=ld)
            p.dma("sync", v8s[:], v8[:, hsl], inc=ld)
            p.add("scalar",
                  (lambda: lambda e: e.activation(
                      out=v8s[:], in_=v8s[:], func=AF.Ln, bias=epsb[:]))(),
                  waits=[(ld, ld.v), (prp, prp.v)], inc=prp)
            p.add("vector",
                  (lambda: lambda e: e.tensor_scalar(
                      m8s[:], m8s[:], -1.0, None, op0=ALU.mult))(),
                  waits=[(ld, ld.v)], inc=prp)
            p.add("scalar",
                  (lambda: lambda e: e.activation(
                      out=r8x[:], in_=v8s[:], func=AF.Exp, scale=-0.5))(),
                  waits=[(prp, prp.v)], inc=prp)
            p.add("gpsimd",
                  (lambda: lambda e: e.tensor_mul(m8s[:], m8s[:], r8x[:]))(),
                  waits=[(prp, prp.v)], inc=prp)
            p.dma("gpsimd", scr_r[:, hsl], r8x[:], waits=[(prp, prp.v)], inc=rup)
            p.dma("gpsimd", scr_mrn[:, hsl], m8s[:], inc=rup)
        rup_all = rup.v

        wts = [wq_t, wk_t, wv_t]
        first_uv = True
        for m in range(3):
            for f in range(16):
                i = (m * 16 + f) % 2
                wwait = ([(uvm, uvm.v), (wsd, wsd.v)] if (m * 16 + f) >= 2 else [])
                p.dma("sync", w32[i][:], wts[m][128 * f:128 * f + 128, :],
                      waits=wwait, inc=wl)
                p.add("vector",
                      (lambda i=i, f=f: lambda e: e.tensor_scalar(
                          wsc[i][:], w32[i][:], onepca[:, f:f + 1], None,
                          op0=ALU.mult))(),
                      waits=[(wl, wl.v), (prp, onepca_ready)], inc=prp)
                p.dma("gpsimd", ws_d[m].rearrange("a p n -> p a n")[:, f, :],
                      wsc[i].ap(), waits=[(prp, prp.v)], inc=wsd)
                for oc in range(2):
                    col = 2 * m + oc
                    p.add("tensor",
                          (lambda i=i, col=col, oc=oc, fu=first_uv,
                           lst=(m == 2 and f == 15 and oc == 1): lambda e: e.matmul(
                              S0[:, col:col + 1],
                              wsc[i][:, 128 * oc:128 * oc + 128], ones_bf[:],
                              start=fu, stop=lst))(),
                          waits=[(prp, prp.v)], inc=uvm)
                    first_uv = False
                    p.add("tensor",
                          (lambda i=i, col=col, oc=oc, m=m, f=f: lambda e: e.matmul(
                              S1[:, col:col + 1],
                              w32[i][:, 128 * oc:128 * oc + 128],
                              mod_sb[:, f:f + 1],
                              start=(m == 0 and f == 0 and oc == 0),
                              stop=(m == 2 and f == 15 and oc == 1)))(),
                          waits=[], inc=uvm)
        uv_all = uvm.v
        p.add("vector", lambda e: e.tensor_copy(u_sb[:], S0[:, 0:6]),
              waits=[(uvm, uv_all)], inc=prp)
        p.add("vector", lambda e: e.tensor_add(v_sb[:], S1[:, 0:6], b_sb[:]),
              inc=prp)
        uv_done = prp.v

        xb_free = [0] * 16
        x32_free = [0] * 2
        r_free = [0, 0]
        mrn_free = [0, 0]
        wg_free = [0, 0]
        ps_free = [0, 0, 0, 0]
        ep_free = [0, 0]
        vt_free = [0, 0]
        ost_free = [0, 0]
        for T in range(8):
            tp = T % 2
            p.dma("gpsimd", r_rep[tp][:],
                  scr_r[T:T + 1, :].partition_broadcast(128),
                  waits=[(rup, rup_all), (csd, r_free[tp])], inc=bcd)
            p.dma("gpsimd", mrn_rep[tp][:],
                  scr_mrn[T:T + 1, :].partition_broadcast(128),
                  waits=[(epd, mrn_free[tp])], inc=bcd)
            bc_here = bcd.v
            for K in range(16):
                xi = (T * 16 + K) % 2
                p.dma("sync", x32[xi][:],
                      x_fm.rearrange("(k q) n -> q k n", q=128)
                      [:, K, NT * T:NT * T + NT],
                      waits=[(csd, x32_free[xi])], inc=xl)
                p.add("vector",
                      (lambda K=K, xi=xi, tp=tp: lambda e: e.tensor_tensor(
                          xb[K][:], x32[xi][:], r_rep[tp][:], op=ALU.mult))(),
                      waits=[(xl, xl.v), (bcd, bc_here - 16),
                             (qmm, xb_free[K]), (prp, uv_done)], inc=csd)
                x32_free[xi] = csd.v
            csd_T = csd.v
            r_free[tp] = csd_T
            for o6 in range(6):
                m, oc = divmod(o6, 2)
                gi = o6 % 2
                psi = o6 % 4
                ps = (S0, S1, O0, O1)[psi]
                p.dma("sync", wgrp[gi].ap().rearrange("q (a n) -> q a n", a=16),
                      ws_d[m].rearrange("a p n -> p a n")
                      [:, :, 128 * oc:128 * oc + 128],
                      waits=[(wsd, wsd.v), (qmm, wg_free[gi])], inc=wgl)
                wg_here = wgl.v
                for f in range(16):
                    for half in range(2):
                        p.add("tensor",
                              (lambda ps=ps, f=f, half=half, gi=gi: lambda e:
                               e.matmul(
                                  ps[:, 512 * half:512 * half + 512],
                                  wgrp[gi][:, 128 * f:128 * f + 128],
                                  xb[f][:, 512 * half:512 * half + 512],
                                  start=(f == 0), stop=(f == 15)))(),
                              waits=([(csd, csd_T), (wgl, wg_here),
                                      (epd, ps_free[psi])]
                                     if f == 0 and half == 0 else []),
                              inc=qmm)
                wg_free[gi] = qmm.v
                for k in range(16):
                    xb_free[k] = qmm.v
                ei = o6 % 2
                p.add("vector",
                      (lambda ps=ps, ei=ei, tp=tp, col=o6: lambda e:
                       e.scalar_tensor_tensor(
                          ep_t[ei][:], in0=mrn_rep[tp][:],
                          scalar=u_sb[:, col:col + 1],
                          in1=ps[:], op0=ALU.mult, op1=ALU.add))(),
                      waits=[(qmm, qmm.v), (epa, ep_free[ei]),
                             (bcd, bc_here)], inc=epd)
                ps_free[psi] = epd.v
                if m < 2:
                    dst = (q_sb if m == 0 else k_sb)[oc]
                    p.add("scalar",
                          (lambda dst=dst, ei=ei, T=T, col=o6: lambda e:
                           e.activation(
                              out=dst[:, NT * T:NT * T + NT], in_=ep_t[ei][:],
                              func=AF.Identity, bias=v_sb[:, col:col + 1]))(),
                          waits=[(epd, epd.v)], inc=epa)
                    ep_free[ei] = epa.v
                else:
                    vi = oc
                    p.add("scalar",
                          (lambda ei=ei, vi=vi, col=o6: lambda e: e.activation(
                              out=vtmp[vi][:], in_=ep_t[ei][:],
                              func=AF.Identity, bias=v_sb[:, col:col + 1]))(),
                          waits=[(epd, epd.v), (vtd, vt_free[vi])], inc=epa)
                    ep_free[ei] = epa.v
                    for j in range(8):
                        p.dma("scalar",
                              v_tm[oc][:, NT * T + 128 * j:NT * T + 128 * j + 128],
                              vtmp[vi][:, 128 * j:128 * j + 128],
                              waits=([(epa, epa.v)] if j == 0 else []), inc=vtd,
                              transpose=True)
                    vt_free[vi] = vtd.v
            mrn_free[tp] = epd.v
        qkv_done_act = epa.v
        qkv_done_vt = vtd.v

        p_bf4 = [[xb[4 * h + j][:] for j in range(4)] for h in range(2)]
        acch = [ep_t[0], ep_t[1]]
        slots = []
        for t in range(8):
            i0 = NT * t
            bmin = max(0, 8 * t - 4)
            bs = list(range(63, bmin - 1, -1))
            if bmin > 0:
                bs.append(0)
            nb = len(bs)
            for idx, b in enumerate(bs):
                is_sink = (b == 0 and bmin > 0)
                is_t0b0 = (t == 0 and b == 0)
                npr = NT if (is_sink or is_t0b0) else min(NT, 128 * (b + 5) - i0)
                for h in range(2):
                    slots.append(dict(kind="blk", t=t, i0=i0, b=b, h=h, idx=idx,
                                      nb=nb, is_sink=is_sink, is_t0b0=is_t0b0,
                                      npr=npr,
                                      crossing=(not is_sink) and
                                      (128 * (b + 4) - i0 < NT)))
            slots.append(dict(kind="tile_end", t=t, i0=i0))

        s_free = [0, 0]
        pbf_free = {}
        o_ps_free = [0, 0]
        ost_free = [0, 0]
        sp_ctr = 0
        LAG = 2

        def emit_front(sl):
            nonlocal_ns = sl
            sp = sl["sp"]
            i0, b, h, npr = sl["i0"], sl["b"], sl["h"], sl["npr"]
            pieces = [(j * 512, min(512, npr - j * 512))
                      for j in range((npr + 511) // 512)]
            sl["pieces"] = pieces
            for pi, (c0, cw) in enumerate(pieces):
                p.add("tensor",
                      (lambda sp=sp, h=h, b=b, i0=i0, c0=c0, cw=cw:
                       lambda e: e.matmul(
                          SP[sp][:, c0:c0 + cw],
                          k_sb[h][:, 128 * b:128 * b + 128],
                          q_sb[h][:, i0 + c0:i0 + c0 + cw],
                          start=True, stop=True))(),
                      waits=([(exd, s_free[sp]), (epa, qkv_done_act)]
                             if pi == 0 else []),
                      inc=smm)
            if sl["crossing"]:
                col0 = 128 * (b + 4) - i0
                mt = tris if b == 0 else tri
                p.add("vector",
                      (lambda sp=sp, col0=col0, mt=mt: lambda e: e.tensor_add(
                          SP[sp][:, col0:col0 + 128],
                          SP[sp][:, col0:col0 + 128], mt[:]))(),
                      waits=[(smm, smm.v)], inc=trid)
            ewaits = ([(trid, trid.v)] if sl["crossing"] else [(smm, smm.v)])
            pbkey = (h, sl["idx"] % 4)
            sl["pbkey"] = pbkey
            pb = p_bf4[h][sl["idx"] % 4]
            sl["pb"] = pb
            pv_f, dn_f = pbf_free.get(pbkey, (0, 0))
            ewaits += [(pvm, pv_f), (dnv, dn_f)]
            if sl["is_t0b0"]:
                p.add("scalar",
                      (lambda sp=sp, pb=pb: lambda e: e.activation(
                          out=pb[:, 0:640], in_=SP[sp][:, 0:640],
                          func=AF.Exp, bias=biasC[:], scale=ASCALE))(),
                      waits=ewaits)
                p.add("scalar",
                      (lambda sp=sp, pb=pb: lambda e: e.activation(
                          out=pb[:, 640:NT], in_=SP[sp][:, 640:NT],
                          func=AF.Exp, bias=biasCs[:], scale=ASCALE))(),
                      inc=exd)
            else:
                bias_t = biasCs if sl["is_sink"] else biasC
                p.add("scalar",
                      (lambda sp=sp, pb=pb, npr=npr, bias_t=bias_t:
                       lambda e: e.activation(
                          out=pb[:, 0:npr], in_=SP[sp][:, 0:npr],
                          func=AF.Exp, bias=bias_t[:], scale=ASCALE))(),
                      waits=ewaits, inc=exd)
            sl["exd_v"] = exd.v
            s_free[sp] = exd.v
            if npr < NT:
                p.add("vector",
                      (lambda pb=pb, npr=npr: lambda e: e.memset(
                          pb[:, npr:NT], 0.0))(),
                      waits=[(exd, exd.v)], inc=dnv)
            sl["dnv_tail"] = dnv.v

        def emit_back(sl):
            h, b, idx, nb = sl["h"], sl["b"], sl["idx"], sl["nb"]
            pb = sl["pb"]
            for pi, (c0, cw) in enumerate(sl["pieces"]):
                p.add("tensor",
                      (lambda h=h, b=b, pb=pb, c0=c0, cw=cw,
                       stt=(idx == 0),
                       stp=(idx == nb - 1 and c0 + cw >= NT): lambda e:
                       e.matmul(
                          OP[h][:, c0:c0 + cw],
                          v_tm[h][:, 128 * b:128 * b + 128],
                          pb[:, c0:c0 + cw],
                          start=stt, stop=stp))(),
                      waits=(([(exd, sl["exd_v"]), (vtd, qkv_done_vt)] +
                              ([(dnv, o_ps_free[h])] if idx == 0 else []))
                             if pi == 0 else []),
                      inc=pvm)
            if idx % 2 == 1:
                j0 = (idx - 1) % 4
                p.add("vector",
                      (lambda h=h, j0=j0: lambda e: e.tensor_add(
                          p_bf4[h][j0], p_bf4[h][j0], p_bf4[h][j0 + 1]))(),
                      waits=[(exd, sl["exd_v"]), (dnv, sl["dnv_tail"]),
                             (pvm, pvm.v)], inc=dnv)
                p.add("vector",
                      (lambda h=h, j0=j0: lambda e: e.tensor_add(
                          acch[h][:], acch[h][:], p_bf4[h][j0]))(),
                      inc=dnv)
                pbf_free[(h, j0)] = (pvm.v, dnv.v)
                pbf_free[(h, j0 + 1)] = (pvm.v, dnv.v)
            elif idx == nb - 1:
                p.add("vector",
                      (lambda h=h, pb=pb: lambda e: e.tensor_add(
                          acch[h][:], acch[h][:], pb[:]))(),
                      waits=[(exd, sl["exd_v"]), (dnv, sl["dnv_tail"]),
                             (dnv, dnv.v)], inc=dnv)
                pbf_free[sl["pbkey"]] = (pvm.v, dnv.v)

        def emit_tile_end(sl):
            i0 = sl["i0"]
            for h in range(2):
                p.add("vector",
                      (lambda h=h: lambda e: e.tensor_copy(
                          ost_b[h][:], OP[h][:]))(),
                      waits=[(pvm, pvm.v), (ost, ost_free[h])], inc=oev)
                p.dma("gpsimd", o_raw[128 * h:128 * h + 128, i0:i0 + NT],
                      ost_b[h].ap(), waits=[(oev, oev.v)], inc=ost)
                ost_free[h] = ost.v
                for half in range(2):
                    p.add("tensor",
                          (lambda h=h, half=half: lambda e: e.matmul(
                              OP[h][0:1, 512 * half:512 * half + 512],
                              ones_f32[:],
                              acch[h][:, 512 * half:512 * half + 512],
                              start=True, stop=True))(),
                          waits=([(oev, oev.v), (dnv, dnv.v)]
                                 if half == 0 else []),
                          inc=dnm)
                p.add("vector",
                      (lambda h=h: lambda e: e.tensor_copy(
                          dd_c[:], OP[h][0:1, :]))(),
                      waits=[(dnm, dnm.v)], inc=dnv)
                p.add("scalar", lambda e: e.activation(
                    out=dd_c[:], in_=dd_c[:], func=AF.Ln),
                    waits=[(dnv, dnv.v), (st, st.v)], inc=dnv)
                p.add("scalar", lambda e: e.activation(
                    out=invd_c[:], in_=dd_c[:], func=AF.Exp, scale=-1.0),
                    inc=dnv)
                p.dma("gpsimd", o_invd[h:h + 1, i0:i0 + NT], invd_c[:],
                      waits=[(dnv, dnv.v)], inc=st)
                o_ps_free[h] = dnv.v
                p.add("vector",
                      (lambda h=h: lambda e: e.memset(acch[h][:], 0.0))(),
                      waits=[(dnm, dnm.v), (epa, qkv_done_act)], inc=dnv)

        for h in range(2):
            p.add("vector",
                  (lambda h=h: lambda e: e.memset(acch[h][:], 0.0))(),
                  waits=[(epa, qkv_done_act)], inc=dnv)
        pend = []
        for sl in slots:
            if sl["kind"] == "tile_end":
                while pend:
                    emit_back(pend.pop(0))
                emit_tile_end(sl)
                continue
            sl["sp"] = sp_ctr % 2
            sp_ctr += 1
            emit_front(sl)
            pend.append(sl)
            if len(pend) > LAG:
                emit_back(pend.pop(0))
        while pend:
            emit_back(pend.pop(0))
        p.add("gpsimd", lambda e: e.wait_ge(st.h, st.v) and None)
        p.add("gpsimd", lambda e: e.wait_ge(ost.h, ost.v) and None)

        p.emit(nc, ctx)
    return nc


def build_l3():
    nc = bass.Bass("TRN2", target_bir_lowering=False)
    o_sl = nc.dram_tensor("o_sl", [DM, NT], BF16, kind="ExternalInput").ap()
    invd_sl = nc.dram_tensor("invd_sl", [16, NT], F32, kind="ExternalInput").ap()
    x_sl = nc.dram_tensor("x_sl", [DM, NT], F32, kind="ExternalInput").ap()
    wo_t = nc.dram_tensor("wo_t", [DM, DM], F32, kind="ExternalInput").ap()
    bo_pm = nc.dram_tensor("bo_pm", [128, 16], F32, kind="ExternalInput").ap()
    mod_all = nc.dram_tensor("mod_all", [128, 96], F32, kind="ExternalInput").ap()
    wdq_all = nc.dram_tensor("wdq_all", [DM, DM], BF16, kind="ExternalInput").ap()
    ident = nc.dram_tensor("ident", [128, 128], F32, kind="ExternalInput").ap()
    out_sl = nc.dram_tensor("out_sl", [DM, NT], F32, kind="ExternalOutput").ap()
    if L3_DEBUG:
        o_x1 = nc.dram_tensor("o_x1", [DM, NT], F32, kind="ExternalOutput").ap()
        o_h2q = nc.dram_tensor("o_h2q", [DM, NT], BF16, kind="ExternalOutput").ap()
        o_scl = nc.dram_tensor("o_scl", [8, DM], F32, kind="ExternalOutput").ap()
        o_g = nc.dram_tensor("o_g", [128, 32], F32, kind="ExternalOutput").ap()
    scr_r2 = nc.dram_tensor("scr_r2", [1, NT], F32, kind="Internal").ap()
    scr_m2 = nc.dram_tensor("scr_m2", [1, NT], F32, kind="Internal").ap()
    scr_rs = nc.dram_tensor("scr_rs", [8, DM], F32, kind="Internal").ap()
    scr_s2 = nc.dram_tensor("scr_s2", [8, DM], F32, kind="Internal").ap()

    p = Prog()
    ld = p.sem("ld")
    prp = p.sem("prp")
    ivb = p.sem("ivb")
    onr = p.sem("onr")
    wol = p.sem("wol")
    wob = p.sem("wob")
    omm = p.sem("omm")
    epa = p.sem("epa")
    epv = p.sem("epv")
    xld = p.sem("xld")
    sqd = p.sem("sqd")
    x1c = p.sem("x1c")
    smm = p.sem("smm")
    rwu = p.sem("rwu")
    rbc = p.sem("rbc")
    h2d = p.sem("h2d")
    trd = p.sem("trd")
    amx_s = p.sem("amx_s")
    qsu = p.sem("qsu")
    qbc = p.sem("qbc")
    qdd = p.sem("qdd")
    wfl = p.sem("wfl")
    fmm = p.sem("fmm")
    fea = p.sem("fea")
    fev = p.sem("fev")
    st = p.sem("st")

    with ExitStack() as ctx:
        def sbt(name, shape, dtype=F32):
            return ctx.enter_context(nc.sbuf_tensor(name, shape, dtype))
        o_sb = sbt("o_sb", [128, 16 * NT], BF16)
        x1 = sbt("x1", [128, 16 * NT])
        h2q = sbt("h2q", [128, 16 * NT], BF16)
        wo32 = [sbt(f"wo32_{i}", [128, 2 * NT]) for i in range(2)]
        wo_bf = [sbt(f"wo_bf{i}", [128, 2 * NT], BF16) for i in range(2)]
        wdqb = None
        wdqb = wo_bf
        inv_rep = [sbt(f"inv_rep{i}", [128, NT]) for i in range(2)]
        x32 = [sbt(f"x32_{i}", [128, NT]) for i in range(2)]
        ep = [sbt(f"ep{i}", [128, NT]) for i in range(2)]
        sqb = [sbt(f"sqb{i}", [128, NT], BF16) for i in range(2)]
        x1b = [sbt(f"x1b{i}", [128, NT], BF16) for i in range(2)]
        h2t = [sbt(f"h2t{i}", [128, NT]) for i in range(2)]
        th1 = x32
        rs_rep = [sbt(f"rs_rep{i}", [128, NT]) for i in range(2)]
        s2_rep = [wo32[0][:, 0:NT], wo32[1][:, 0:NT]]
        yq = ep
        q8 = [sbt(f"q8_{i}", [128, NT], FP8) for i in range(2)]
        out_sb = h2t
        rs2T = [sbt(f"rs2T{i}", [8, 128]) for i in range(2)]
        s2T = [sbt(f"s2T{i}", [8, 128]) for i in range(2)]
        amxk = sbt("amxk", [128, 8])
        rs2k = sbt("rs2k", [128, 8])
        rcpk = sbt("rcpk", [128, 8])
        mod_sb = sbt("mod_sb", [128, 96])
        ga_fm = sbt("ga_fm", [128, 16])
        gb_fm = sbt("gb_fm", [128, 16])
        gabo = sbt("gabo", [128, 16])
        cb1 = sbt("cb1", [128, 16])
        tsg = sbt("tsg", [128, 16])
        bo_sb = sbt("bo_sb", [128, 16])
        idt = sbt("idt", [128, 128])
        ones_bf = sbt("ones_bf", [128, 1], BF16)
        m2v = inv_rep[0][0:1, :]
        v2v = inv_rep[1][0:1, :]
        t2v = x32[0][0:1, :]
        r2x = x32[1][0:1, :]
        epsb = sbt("epsb", [1, 1])

        P0 = ctx.enter_context(nc.psum_tensor("P0", [128, NT], F32))
        P1 = ctx.enter_context(nc.psum_tensor("P1", [128, NT], F32))
        P2 = ctx.enter_context(nc.psum_tensor("P2", [128, NT], F32))
        P3 = ctx.enter_context(nc.psum_tensor("P3", [128, NT], F32))

        p.dma("sync", mod_sb[:], mod_all, inc=ld)
        p.dma("sync", bo_sb[:], bo_pm, inc=ld)
        p.dma("sync", idt[:], ident, inc=ld)
        p.dma("sync", o_sb.ap().rearrange("q (k n) -> q k n", k=16),
              o_sl.rearrange("(k q) n -> q k n", q=128), inc=ld)
        ld_all = ld.v
        p.add("vector", lambda e: e.memset(epsb[:], EPS), inc=prp)
        p.add("vector", lambda e: e.memset(ones_bf[:], 1.0), inc=prp)
        for (dst, col) in ((ga_fm, 32), (gb_fm, 80)):
            p.add("scalar",
                  (lambda dst=dst, col=col: lambda e: e.activation(
                      out=tsg[:], in_=mod_sb[:, col:col + 16], func=AF.Sign))(),
                  waits=[(ld, ld_all), (prp, prp.v)], inc=prp)
            p.add("scalar",
                  (lambda dst=dst, col=col: lambda e: e.activation(
                      out=dst[:], in_=mod_sb[:, col:col + 16], func=AF.Abs))(),
                  inc=prp)
            p.add("vector",
                  (lambda dst=dst: lambda e: e.tensor_scalar(
                      dst[:], dst[:], 0.25, None, op0=ALU.max))(),
                  waits=[(prp, prp.v)], inc=prp)
            p.add("gpsimd",
                  (lambda dst=dst: lambda e: e.tensor_mul(
                      dst[:], dst[:], tsg[:]))(),
                  waits=[(prp, prp.v)], inc=prp)
        p.add("gpsimd", lambda e: e.tensor_mul(gabo[:], ga_fm[:], bo_sb[:]),
              waits=[(prp, prp.v), (ld, ld_all)], inc=prp)
        p.add("vector", lambda e: e.tensor_scalar(
            cb1[:], mod_sb[:, 64:80], 1.0, None, op0=ALU.add),
            waits=[(ld, ld_all)], inc=prp)
        gates_done = prp.v

        iv_free = [0, 0]
        for K in range(16):
            ki = K % 2
            p.dma("gpsimd", inv_rep[ki][:],
                  invd_sl[K:K + 1, :].partition_broadcast(128),
                  waits=[(onr, iv_free[ki])], inc=ivb)
            p.add("vector",
                  (lambda K=K, ki=ki: lambda e: e.tensor_tensor(
                      o_sb[:, NT * K:NT * K + NT], o_sb[:, NT * K:NT * K + NT],
                      inv_rep[ki][:], op=ALU.mult))(),
                  waits=[(ivb, ivb.v), (ld, ld_all)], inc=onr)
            iv_free[ki] = onr.v
        onorm_done = onr.v
        wo_free = [0, 0]
        wob_free = [0, 0]
        ep_free = [0, 0]
        x32_free = [0, 0]
        ps_free = [0, 0]
        for of in range(16):
            oi = of % 2
            p.dma("sync", wo32[oi].ap().rearrange("q (k n) -> q k n", k=16),
                  wo_t.rearrange("(k q) n -> q k n", q=128)
                  [:, :, 128 * of:128 * of + 128],
                  waits=[(wob, wo_free[oi])], inc=wol)
            p.add("vector",
                  (lambda oi=oi: lambda e: e.tensor_copy(
                      wo_bf[oi][:], wo32[oi][:]))(),
                  waits=[(wol, wol.v), (omm, wob_free[oi])], inc=wob)
            p.dma("sync", x32[oi][:], x_sl[128 * of:128 * of + 128, :],
                  waits=[(epv, x32_free[oi])], inc=xld)
            ps = (P0, P1)[oi]
            for f in range(16):
                for half in range(2):
                    p.add("tensor",
                          (lambda ps=ps, oi=oi, f=f, half=half: lambda e:
                           e.matmul(
                              ps[:, 512 * half:512 * half + 512],
                              wo_bf[oi][:, 128 * f:128 * f + 128],
                              o_sb[:, NT * f + 512 * half:
                                   NT * f + 512 * half + 512],
                              start=(f == 0), stop=(f == 15)))(),
                          waits=([(wob, wob.v), (onr, onorm_done),
                                  (epa, ps_free[oi])]
                                 if f == 0 and half == 0 else []),
                          inc=omm)
            wob_free[oi] = omm.v
            p.add("scalar",
                  (lambda ps=ps, oi=oi, of=of: lambda e: e.activation(
                      out=ep[oi][:], in_=ps[:], func=AF.Identity,
                      scale=ga_fm[:, of:of + 1], bias=gabo[:, of:of + 1]))(),
                  waits=[(omm, omm.v), (prp, gates_done),
                         (epv, ep_free[oi])], inc=epa)
            ps_free[oi] = epa.v
            p.add("vector",
                  (lambda oi=oi, of=of: lambda e: e.tensor_add(
                      x1[:, NT * of:NT * of + NT], ep[oi][:], x32[oi][:]))(),
                  waits=[(epa, epa.v), (xld, xld.v)], inc=epv)
            ep_free[oi] = epv.v
            x32_free[oi] = epv.v
            wo_free[oi] = wob.v
        oproj_done = epv.v

        sq_free = [0, 0]
        x1b_free = [0, 0]
        for K in range(16):
            ki = K % 2
            p.add("scalar",
                  (lambda K=K, ki=ki: lambda e: e.activation(
                      out=sqb[ki][:], in_=x1[:, NT * K:NT * K + NT],
                      func=AF.Square))(),
                  waits=[(epv, oproj_done), (smm, sq_free[ki])], inc=sqd)
            p.add("vector",
                  (lambda K=K, ki=ki: lambda e: e.tensor_copy(
                      x1b[ki][:], x1[:, NT * K:NT * K + NT]))(),
                  waits=[(epv, oproj_done), (smm, x1b_free[ki])], inc=x1c)
            for half in range(2):
                p.add("tensor",
                      (lambda K=K, ki=ki, half=half: lambda e: e.matmul(
                          P2[0:1, 512 * half:512 * half + 512],
                          ones_bf[:],
                          x1b[ki][:, 512 * half:512 * half + 512],
                          start=(K == 0), stop=(K == 15)))(),
                      waits=[(x1c, x1c.v), (prp, 2)] if half == 0 else [],
                      inc=smm)
                p.add("tensor",
                      (lambda K=K, ki=ki, half=half: lambda e: e.matmul(
                          P3[0:1, 512 * half:512 * half + 512],
                          ones_bf[:],
                          sqb[ki][:, 512 * half:512 * half + 512],
                          start=(K == 0), stop=(K == 15)))(),
                      waits=[(sqd, sqd.v)] if half == 0 else [], inc=smm)
            sq_free[ki] = smm.v
            x1b_free[ki] = smm.v
        smm_all = smm.v
        p.add("vector", lambda e: e.tensor_scalar(
            m2v, P2[0:1, :], 1.0 / DM, None, op0=ALU.mult),
            waits=[(smm, smm_all)], inc=prp)
        p.add("vector", lambda e: e.tensor_scalar(
            v2v, P3[0:1, :], 1.0 / DM, None, op0=ALU.mult), inc=prp)
        p.add("vector", lambda e: e.tensor_mul(t2v, m2v, m2v),
              waits=[(prp, prp.v)], inc=prp)
        p.add("vector", lambda e: e.memset(sqb[0][:], 0.0), inc=prp)
        p.add("vector", lambda e: e.tensor_sub(v2v, v2v, t2v), inc=prp)
        p.add("scalar", lambda e: e.activation(
            out=v2v, in_=v2v, func=AF.Ln, bias=epsb[:]),
            waits=[(prp, prp.v)], inc=prp)
        p.add("vector", lambda e: e.tensor_scalar(
            m2v, m2v, -1.0, None, op0=ALU.mult), inc=prp)
        p.add("scalar", lambda e: e.activation(
            out=r2x, in_=v2v, func=AF.Exp, scale=-0.5),
            waits=[(prp, prp.v)], inc=prp)
        p.add("gpsimd", lambda e: e.tensor_mul(m2v, m2v, r2x),
            waits=[(prp, prp.v)], inc=prp)
        rows2_done = prp.v
        p.dma("gpsimd", scr_r2, r2x, waits=[(prp, rows2_done)], inc=rwu)
        p.dma("gpsimd", scr_m2, m2v, inc=rwu)
        rwu_all = rwu.v
        p.dma("gpsimd", rs_rep[0][:], scr_r2.partition_broadcast(128),
              waits=[(rwu, rwu_all)], inc=rbc)
        p.dma("gpsimd", rs_rep[1][:], scr_m2.partition_broadcast(128), inc=rbc)
        r2bc_done = rbc.v
        r2_rep = rs_rep[0]
        m2n_rep = rs_rep[1]

        h2_free = [0, 0]
        th_free = [0, 0]
        qrep_free = [0, 0]
        amx_war = 0
        for K in range(16):
            ki = K % 2
            p.add("vector",
                  (lambda K=K, ki=ki: lambda e: e.tensor_tensor(
                      th1[ki][:], x1[:, NT * K:NT * K + NT], r2_rep[:],
                      op=ALU.mult))(),
                  waits=[(rbc, r2bc_done), (h2d, th_free[ki])], inc=h2d)
            p.add("vector",
                  (lambda ki=ki: lambda e: e.scalar_tensor_tensor(
                      th1[ki][:], in0=m2n_rep[:], scalar=1.0, in1=th1[ki][:],
                      op0=ALU.mult, op1=ALU.add))(),
                  inc=h2d)
            p.add("scalar",
                  (lambda K=K, ki=ki: lambda e: e.activation(
                      out=h2t[ki][:], in_=th1[ki][:], func=AF.Identity,
                      scale=cb1[:, K:K + 1],
                      bias=mod_sb[:, 48 + K:48 + K + 1]))(),
                  waits=[(h2d, h2d.v), (qdd, h2_free[ki])], inc=h2d)
            th_free[ki] = h2d.v
            amx_prev = [amx_s.v, amx_s.v]
            for j in range(8):
                ji = j % 2
                psl = P2[:, 512 * ji:512 * ji + 128]
                p.add("tensor",
                      (lambda ki=ki, j=j, psl=psl: lambda e: e.transpose(
                          psl, h2t[ki][:, 128 * j:128 * j + 128], idt[:]))(),
                      waits=[(h2d, h2d.v), (ld, ld_all),
                             (amx_s, amx_prev[ji])],
                      inc=trd)
                p.add("vector",
                      (lambda j=j, psl=psl: lambda e: e.tensor_reduce(
                          amxk[:, j:j + 1], psl, axis=AX.X,
                          op=ALU.max, apply_absolute_value=True))(),
                      waits=[(trd, trd.v)] +
                            ([(trd, amx_war)] if j == 0 else []),
                      inc=amx_s)
                amx_prev[ji] = amx_s.v
            p.add("scalar", lambda e: e.activation(
                out=rs2k[:], in_=amxk[:], func=AF.Ln, scale=1.0 / 224.0),
                waits=[(amx_s, amx_s.v)], inc=amx_s)
            p.add("scalar", lambda e: e.activation(
                out=rcpk[:], in_=rs2k[:], func=AF.Exp, scale=-1.0),
                inc=amx_s)
            p.add("scalar", lambda e: e.activation(
                out=amxk[:], in_=rs2k[:], func=AF.Exp, scale=1.0),
                inc=amx_s)
            p.add("tensor",
                  (lambda: lambda e: e.transpose(
                      P2[0:8, 0:128], rcpk[:, 0:8], idt[:]))(),
                  waits=[(amx_s, amx_s.v)], inc=trd)
            p.add("tensor",
                  (lambda: lambda e: e.transpose(
                      P2[0:8, 512:640], amxk[:, 0:8], idt[:]))(),
                  inc=trd)
            p.add("vector",
                  (lambda ki=ki: lambda e: e.tensor_copy(
                      rs2T[ki][:], P2[0:8, 0:128]))(),
                  waits=[(trd, trd.v), (qsu, qsu.v)], inc=amx_s)
            p.add("vector",
                  (lambda ki=ki: lambda e: e.tensor_copy(
                      s2T[ki][:], P2[0:8, 512:640]))(),
                  inc=amx_s)
            amx_war = trd.v
            p.dma("gpsimd", scr_rs[:, 128 * K:128 * K + 128],
                  rs2T[ki][:], waits=[(amx_s, amx_s.v)], inc=qsu)
            p.dma("gpsimd", scr_s2[:, 128 * K:128 * K + 128],
                  s2T[ki][:], inc=qsu)
            qsu_here = qsu.v
            p.dma("sync", yq[ki][:],
                  scr_rs[:, 128 * K:128 * K + 128].partition_broadcast(128),
                  waits=[(qsu, qsu_here), (qdd, qrep_free[ki])], inc=qbc)
            p.dma("sync", s2_rep[ki],
                  scr_s2[:, 128 * K:128 * K + 128].partition_broadcast(128),
                  inc=qbc)
            qbc_here = qbc.v
            p.add("vector",
                  (lambda ki=ki: lambda e: e.tensor_tensor(
                      yq[ki][:], h2t[ki][:], yq[ki][:], op=ALU.mult))(),
                  waits=[(qbc, qbc_here)], inc=qdd)
            p.add("vector",
                  (lambda ki=ki: lambda e: e.tensor_copy(q8[ki][:], yq[ki][:]))(),
                  inc=qdd)
            p.add("vector",
                  (lambda K=K, ki=ki: lambda e: e.tensor_tensor(
                      h2q[:, NT * K:NT * K + NT], q8[ki][:], s2_rep[ki],
                      op=ALU.mult))(),
                  inc=qdd)
            h2_free[ki] = qdd.v
            qrep_free[ki] = qdd.v
        quant_done = qdd.v
        if L3_DEBUG:
            p.dma("gpsimd", o_g[:, 0:16], ga_fm.ap(), waits=[(prp, gates_done)],
                  inc=st)
            p.dma("gpsimd", o_g[:, 16:32], gb_fm.ap(), inc=st)
            p.dma("gpsimd", o_x1.rearrange("(k q) n -> q k n", q=128),
                  x1.ap().rearrange("q (k n) -> q k n", k=16),
                  waits=[(epv, oproj_done)], inc=st)
            p.dma("gpsimd", o_h2q.rearrange("(k q) n -> q k n", q=128),
                  h2q.ap().rearrange("q (k n) -> q k n", k=16),
                  waits=[(qdd, quant_done)], inc=st)
            p.dma("gpsimd", o_scl, scr_rs, waits=[(qsu, qsu.v)], inc=st)

        wf_free = [0, 0]
        fps_free = [0, 0]
        fep_free = [0, 0]
        fout_free = [0, 0]
        for of in range(16):
            oi = of % 2
            p.dma("sync", wdqb[oi].ap().rearrange("q (k n) -> q k n", k=16),
                  wdq_all.rearrange("(k q) n -> q k n", q=128)
                  [:, :, 128 * of:128 * of + 128],
                  waits=[(fmm, wf_free[oi])], inc=wfl)
            wfl_here = wfl.v
            ps = (P0, P1)[oi]
            for f in range(16):
                for half in range(2):
                    p.add("tensor",
                          (lambda ps=ps, oi=oi, f=f, half=half: lambda e:
                           e.matmul(
                              ps[:, 512 * half:512 * half + 512],
                              wdqb[oi][:, 128 * f:128 * f + 128],
                              h2q[:, NT * f + 512 * half:
                                   NT * f + 512 * half + 512],
                              start=(f == 0), stop=(f == 15)))(),
                          waits=([(wfl, wfl_here), (qdd, quant_done),
                                  (fea, fps_free[oi])]
                                 if f == 0 and half == 0 else []),
                          inc=fmm)
            wf_free[oi] = fmm.v
            p.add("scalar",
                  (lambda ps=ps, oi=oi, of=of: lambda e: e.activation(
                      out=ep[oi][:], in_=ps[:], func=AF.Identity,
                      scale=gb_fm[:, of:of + 1]))(),
                  waits=[(fmm, fmm.v), (fev, fep_free[oi])], inc=fea)
            fps_free[oi] = fea.v
            p.add("vector",
                  (lambda oi=oi, of=of: lambda e: e.tensor_add(
                      out_sb[oi][:], ep[oi][:],
                      x1[:, NT * of:NT * of + NT]))(),
                  waits=[(fea, fea.v), (st, fout_free[oi])], inc=fev)
            fep_free[oi] = fev.v
            p.dma("gpsimd", out_sl[128 * of:128 * of + 128, :],
                  out_sb[oi].ap(), waits=[(fev, fev.v)], inc=st)
            fout_free[oi] = st.v
        p.add("gpsimd", lambda e: e.wait_ge(st.h, st.v) and None)

        p.emit(nc, ctx)
    return nc


def _bf(x):
    return np.ascontiguousarray(x, dtype=ml_dtypes.bfloat16)


def _f32(x):
    return np.ascontiguousarray(x, dtype=np.float32)


_NC_CACHE = {}


def _get_nc(name):
    if name not in _NC_CACHE:
        _NC_CACHE[name] = {"l1": build_l1, "l2": build_l2, "l3": build_l3}[name]()
    return _NC_CACHE[name]


def kernel(**inputs):
    x = np.asarray(inputs["L0_attn_hidden_states"])[0]
    x_fm = np.ascontiguousarray(x.T)
    cvec = np.ascontiguousarray(np.asarray(inputs["L0_c"]))
    wmod = np.asarray(inputs["L0_wmod"])
    wffn = np.asarray(inputs["L0_ffn_weight"])
    wffn_t = np.ascontiguousarray(wffn.T)
    Wq_t = np.ascontiguousarray(np.asarray(inputs["L0_attn_Wq"]).T)
    Wk_t = np.ascontiguousarray(np.asarray(inputs["L0_attn_Wk"]).T)
    Wv_t = np.ascontiguousarray(np.asarray(inputs["L0_attn_Wv"]).T)
    Wo_t = np.ascontiguousarray(np.asarray(inputs["L0_attn_Wo"]).T)
    bq = np.asarray(inputs["L0_attn_bq"])
    bk = np.asarray(inputs["L0_attn_bk"])
    bv = np.asarray(inputs["L0_attn_bv"])
    bo = np.asarray(inputs["L0_attn_bo"])
    ident = np.eye(128, dtype=np.float32)
    cores = list(range(NCORE))

    im1 = []
    for c in cores:
        im1.append({
            "x_sl": np.ascontiguousarray(x_fm[:, NT * c:NT * (c + 1)]),
            "c_in": cvec,
            "wmod_sl": np.ascontiguousarray(wmod[:, 1536 * c:1536 * (c + 1)]),
            "wfn": np.ascontiguousarray(wffn[256 * c:256 * (c + 1), :]),
            "wft": np.ascontiguousarray(wffn_t[:, 256 * c:256 * (c + 1)]),
            "ident": ident,
        })
    r1 = run_bass_kernel_spmd(_get_nc("l1"), im1, core_ids=cores).results

    mod_all = np.concatenate([r1[c]["o_mod"] for c in cores], axis=1)
    m8 = np.stack([r1[c]["o_stats"][0] for c in cores], axis=0)
    v8 = np.stack([r1[c]["o_stats"][1] for c in cores], axis=0)
    wdq_all = np.concatenate([r1[c]["o_wdq"] for c in cores], axis=1)

    jidx = np.arange(128)
    tri = np.where(jidx[:, None] < jidx[None, :], np.float32(NEG),
                   np.float32(0)).astype(np.float32)
    tris = np.where((jidx[:, None] >= SINKS) & (jidx[:, None] < jidx[None, :]),
                    np.float32(NEG), np.float32(0)).astype(np.float32)
    bc = np.full((128, 1), -CSH, dtype=np.float32)
    bcs = bc.copy()
    bcs[SINKS:, 0] = -1.0e38
    im2 = []
    for c in cores:
        sl = slice(256 * c, 256 * c + 256)
        b_pm = np.stack([bq[sl].reshape(2, 128).T, bk[sl].reshape(2, 128).T,
                         bv[sl].reshape(2, 128).T], axis=1).reshape(128, 6)
        im2.append({
            "x_fm": x_fm,
            "wq_t": np.ascontiguousarray(Wq_t[:, sl]),
            "wk_t": np.ascontiguousarray(Wk_t[:, sl]),
            "wv_t": np.ascontiguousarray(Wv_t[:, sl]),
            "b_pm": np.ascontiguousarray(b_pm, dtype=np.float32),
            "mod_all": mod_all, "m8": m8, "v8": v8,
            "tri_in": tri, "tris_in": tris, "bc_in": bc, "bcs_in": bcs,
        })
    r2 = run_bass_kernel_spmd(_get_nc("l2"), im2, core_ids=cores).results

    o_all = np.concatenate([r2[c]["o_raw"] for c in cores], axis=0)
    invd_all = np.concatenate([r2[c]["o_invd"] for c in cores], axis=0)

    bo_pm = np.ascontiguousarray(bo.reshape(16, 128).T, dtype=np.float32)
    im3 = []
    for c in cores:
        tsl = slice(NT * c, NT * (c + 1))
        im3.append({
            "o_sl": np.ascontiguousarray(o_all[:, tsl]),
            "invd_sl": np.ascontiguousarray(invd_all[:, tsl]),
            "x_sl": np.ascontiguousarray(x_fm[:, tsl]),
            "wo_t": Wo_t,
            "bo_pm": bo_pm,
            "mod_all": mod_all,
            "wdq_all": wdq_all,
            "ident": ident,
        })
    r3 = run_bass_kernel_spmd(_get_nc("l3"), im3, core_ids=cores).results
    out_fm = np.concatenate([r3[c]["out_sl"] for c in cores], axis=1)
    return np.ascontiguousarray(out_fm.T)[None, :, :]


# version 04
# speedup vs baseline: 1.5652x; 1.1659x over previous
import math
import numpy as np
import ml_dtypes
from contextlib import ExitStack

import concourse.bass as bass
import concourse.mybir as mybir
from concourse.bass_utils import run_bass_kernel_spmd

F32 = mybir.dt.float32
BF16 = mybir.dt.bfloat16
FP8 = mybir.dt.float8e4
AF = mybir.ActivationFunctionType
ALU = mybir.AluOpType
AX = mybir.AxisListType

NQ = 8192
DM = 2048
NT = 1024
NH = 16
HD = 128
WIN = 512
SINKS = 4
NCORE = 8
CSH = 20.0
ASCALE = 1.0 / math.sqrt(HD)
NEG = -1.0e30
EPS = 1e-5
L3_DEBUG = False
ENGS = ("sync", "vector", "scalar", "tensor", "gpsimd")


class Sem:
    def __init__(self, name):
        self.name = name
        self.v = 0
        self.h = None

    def bump(self, n):
        self.v += n
        return self.v


class Prog:

    def __init__(self):
        self.ops = {e: [] for e in ENGS}
        self.sems = []

    def sem(self, name):
        s = Sem(name)
        self.sems.append(s)
        return s

    def add(self, eng, fn, waits=(), inc=None, inc_n=1):
        planned = None
        if inc is not None:
            planned = inc.bump(inc_n)
        self.ops[eng].append((fn, list(waits), inc, inc_n))
        return planned

    def dma(self, eng, out, in_, waits=(), inc=None, **kw):
        return self.add(eng, lambda e: e.dma_start(out=out, in_=in_, **kw),
                        waits=waits, inc=inc, inc_n=16)

    def emit(self, nc, ctx):
        for s in self.sems:
            s.h = ctx.enter_context(nc.semaphore(s.name))
        block = ctx.enter_context(nc.Block())

        def mk(eng_name):
            def body(eng):
                for fn, waits, inc, inc_n in self.ops[eng_name]:
                    for (s, v) in waits:
                        eng.wait_ge(s.h, v)
                    instr = fn(eng)
                    if inc is not None:
                        assert instr is not None
                        instr.then_inc(inc.h, inc_n)
            return body

        for name in ENGS:
            if not self.ops[name]:
                continue
            getattr(block, name)(mk(name))


def build_l1():
    nc = bass.Bass("TRN2", target_bir_lowering=False)
    x_sl = nc.dram_tensor("x_sl", [DM, NT], F32, kind="ExternalInput").ap()
    c_in = nc.dram_tensor("c_in", [1, DM], F32, kind="ExternalInput").ap()
    wmod_sl = nc.dram_tensor("wmod_sl", [DM, 1536], F32, kind="ExternalInput").ap()
    wfn = nc.dram_tensor("wfn", [256, DM], F32, kind="ExternalInput").ap()
    wft = nc.dram_tensor("wft", [DM, 256], F32, kind="ExternalInput").ap()
    ident = nc.dram_tensor("ident", [128, 128], F32, kind="ExternalInput").ap()
    o_stats = nc.dram_tensor("o_stats", [2, NT], F32, kind="ExternalOutput").ap()
    o_mod = nc.dram_tensor("o_mod", [128, 12], F32, kind="ExternalOutput").ap()
    o_wdq = nc.dram_tensor("o_wdq", [DM, 256], BF16, kind="ExternalOutput").ap()
    scr_rs = nc.dram_tensor("scr_rs", [16, 256], F32, kind="Internal").ap()
    scr_s2 = nc.dram_tensor("scr_s2", [16, 256], F32, kind="Internal").ap()

    p = Prog()
    ld = p.sem("ld")
    sil = p.sem("sil")
    wld = p.sem("wld")
    xld = p.sem("xld")
    cstv = p.sem("cstv")
    csts = p.sem("csts")
    smm = p.sem("smm")
    mmm = p.sem("mmm")
    ved = p.sem("ved")
    sca = p.sem("sca")
    bcd = p.sem("bcd")
    qd = p.sem("qd")
    st = p.sem("st")

    with ExitStack() as ctx:
        def sbt(name, shape, dtype=F32):
            return ctx.enter_context(nc.sbuf_tensor(name, shape, dtype))
        cfm = sbt("cfm", [128, 16])
        scfm = sbt("scfm", [128, 16])
        wmodb = [sbt(f"wmodb{i}", [128, 1536]) for i in range(2)]
        x32 = [sbt(f"x32_{i}", [128, NT]) for i in range(2)]
        xbb = [sbt(f"xbb{i}", [128, NT], BF16) for i in range(2)]
        sqb = [sbt(f"sqb{i}", [128, NT], BF16) for i in range(2)]
        wfns = sbt("wfns", [128, 2 * DM])
        wfts = sbt("wfts", [128, 16 * 256])
        idt = sbt("idt", [128, 128])
        modsb = sbt("modsb", [128, 12])
        ones_bf = sbt("ones_bf", [128, 1], BF16)
        t_mean = sbt("t_mean", [1, NT])
        t_var = sbt("t_var", [1, NT])
        t_m2 = sbt("t_m2", [1, NT])
        amx = sbt("amx", [128, 32])
        sT = sbt("sT", [16, 256])
        rs2T = sbt("rs2T", [16, 256])
        s2T = sbt("s2T", [16, 256])
        repA = [sbt(f"repA{i}", [128, 256]) for i in range(2)]
        repB = [sbt(f"repB{i}", [128, 256]) for i in range(2)]
        yq = [sbt(f"yq{i}", [128, 256]) for i in range(2)]
        q8 = [sbt(f"q8_{i}", [128, 256], FP8) for i in range(2)]
        wdq = sbt("wdq", [128, 16 * 256], BF16)
        epsb = sbt("epsb", [1, 1])

        psmod = ctx.enter_context(nc.psum_tensor("psmod", [128, 12], F32))
        ps_sa = ctx.enter_context(nc.psum_tensor("ps_sa", [1, 512], F32))
        ps_sb = ctx.enter_context(nc.psum_tensor("ps_sb", [1, 512], F32))
        ps_qa = ctx.enter_context(nc.psum_tensor("ps_qa", [1, 512], F32))
        ps_qb = ctx.enter_context(nc.psum_tensor("ps_qb", [1, 512], F32))
        ps_t = ctx.enter_context(nc.psum_tensor("ps_t", [16, 128], F32))

        p.dma("sync", cfm[:], c_in.rearrange("o (k q) -> q (o k)", q=128), inc=ld,
              allow_slow_non_contiguous=True)
        p.dma("sync", idt[:], ident, inc=ld)
        p.dma("sync", wfns.ap().rearrange("q (a n) -> q a n", a=2),
              wfn.rearrange("(a q) n -> q a n", q=128), inc=ld)
        p.dma("sync", wfts.ap().rearrange("q (k n) -> q k n", k=16),
              wft.rearrange("(k q) n -> q k n", q=128), inc=ld)
        ld_all = ld.v

        p.add("scalar", lambda e: e.activation(out=scfm[:], in_=cfm[:], func=AF.Silu),
              waits=[(ld, ld_all)], inc=sil)

        for K in range(16):
            p.dma("sync", wmodb[K % 2][:], wmod_sl[128 * K:128 * K + 128, :],
                  waits=([(mmm, (K - 2) * 12 + 12)] if K >= 2 else []), inc=wld)
            for oc in range(12):
                p.add("tensor",
                      (lambda K=K, oc=oc: lambda e: e.matmul(
                          psmod[:, oc:oc + 1],
                          wmodb[K % 2][:, 128 * oc:128 * oc + 128],
                          scfm[:, K:K + 1],
                          start=(K == 0 and oc == 0),
                          stop=(K == 15 and oc == 11)))(),
                      waits=([(wld, 16 * (K + 1)), (sil, 1)] if oc == 0 else []),
                      inc=mmm)
        p.add("vector", lambda e: e.tensor_copy(modsb[:], psmod[:]),
              waits=[(mmm, mmm.v)], inc=ved)
        p.dma("gpsimd", o_mod, modsb[:], waits=[(ved, ved.v)], inc=st)

        p.add("vector", lambda e: e.memset(ones_bf[:], 1.0), inc=ved)
        ones_ready = ved.v
        for K in range(16):
            w = [(cstv, K - 1), (csts, K - 1)] if K >= 2 else []
            p.dma("sync", x32[K % 2][:], x_sl[128 * K:128 * K + 128, :],
                  waits=w, inc=xld)
            wr = [(smm, 4 * (K - 1))] if K >= 2 else []
            p.add("vector",
                  (lambda K=K: lambda e: e.tensor_copy(xbb[K % 2][:], x32[K % 2][:]))(),
                  waits=[(xld, 16 * (K + 1))] + wr, inc=cstv)
            p.add("scalar",
                  (lambda K=K: lambda e: e.activation(
                      out=sqb[K % 2][:], in_=x32[K % 2][:], func=AF.Square))(),
                  waits=[(xld, 16 * (K + 1))] + wr, inc=csts)
            for h in range(2):
                p.add("tensor",
                      (lambda K=K, h=h: lambda e: e.matmul(
                          (ps_sa if h == 0 else ps_sb)[:],
                          ones_bf[:], xbb[K % 2][:, 512 * h:512 * h + 512],
                          start=(K == 0), stop=(K == 15)))(),
                      waits=[(cstv, K + 1), (ved, ones_ready)], inc=smm)
                p.add("tensor",
                      (lambda K=K, h=h: lambda e: e.matmul(
                          (ps_qa if h == 0 else ps_qb)[:],
                          ones_bf[:], sqb[K % 2][:, 512 * h:512 * h + 512],
                          start=(K == 0), stop=(K == 15)))(),
                      waits=[(csts, K + 1)], inc=smm)
        smm_all = smm.v
        for h, (pss, psq) in enumerate(((ps_sa, ps_qa), (ps_sb, ps_qb))):
            sl = slice(512 * h, 512 * h + 512)
            p.add("vector",
                  (lambda pss=pss, sl=sl: lambda e: e.tensor_scalar(
                      t_mean[0:1, sl], pss[:], 1.0 / DM, None, op0=ALU.mult))(),
                  waits=[(smm, smm_all)], inc=ved)
            p.add("vector",
                  (lambda psq=psq, sl=sl: lambda e: e.tensor_scalar(
                      t_var[0:1, sl], psq[:], 1.0 / DM, None, op0=ALU.mult))(),
                  inc=ved)
            p.add("gpsimd",
                  (lambda sl=sl: lambda e: e.tensor_mul(
                      t_m2[0:1, sl], t_mean[0:1, sl], t_mean[0:1, sl]))(),
                  waits=[(ved, ved.v)], inc=ved)
            p.add("gpsimd",
                  (lambda sl=sl: lambda e: e.tensor_sub(
                      t_var[0:1, sl], t_var[0:1, sl], t_m2[0:1, sl]))(),
                  inc=ved)
        p.dma("gpsimd", o_stats[0:1, :], t_mean[:], waits=[(ved, ved.v)], inc=st)
        p.dma("gpsimd", o_stats[1:2, :], t_var[:], inc=st)

        for i in range(2):
            p.add("vector",
                  (lambda i=i: lambda e: e.tensor_reduce(
                      amx[:, 16 * i:16 * i + 16],
                      wfns.ap().rearrange("q (a k f) -> q (a k) f", a=2, k=16)
                      [:, 16 * i:16 * i + 16, :],
                      axis=AX.X, op=ALU.max, apply_absolute_value=True))(),
                  waits=[(ld, ld_all)], inc=ved)
        amx_done = ved.v
        for i in range(2):
            p.add("tensor",
                  (lambda i=i: lambda e: e.transpose(
                      ps_t[:], amx[:, 16 * i:16 * i + 16], idt[:]))(),
                  waits=[(ved, amx_done), (ld, ld_all)] + ([(ved, ved.v)] if i else []),
                  inc=mmm)
            p.add("vector",
                  (lambda i=i: lambda e: e.tensor_scalar(
                      sT[:, 128 * i:128 * i + 128], ps_t[:], 1.0 / 448.0, None,
                      op0=ALU.mult))(),
                  waits=[(mmm, mmm.v)], inc=ved)
        p.add("vector", lambda e: e.tensor_scalar(s2T[:], sT[:], 2.0, None,
                                                  op0=ALU.mult), inc=ved)
        p.add("scalar", lambda e: e.activation(out=sT[:], in_=s2T[:],
                                               func=AF.Ln),
              waits=[(ved, ved.v)], inc=ved)
        p.add("scalar", lambda e: e.activation(out=rs2T[:], in_=sT[:],
                                               func=AF.Exp, scale=-1.0),
              inc=ved)
        sc_done = ved.v
        p.dma("gpsimd", scr_rs, rs2T[:], waits=[(ved, sc_done)], inc=sca)
        p.dma("gpsimd", scr_s2, s2T[:], inc=sca)
        sca_all = sca.v

        for K in range(16):
            i = K % 2
            w = [(sca, sca_all)] + ([(qd, (K - 2) + 1)] if K >= 2 else [])
            p.dma("sync", repA[i][:], scr_rs[K:K + 1, :].partition_broadcast(128),
                  waits=w, inc=bcd)
            p.dma("sync", repB[i][:], scr_s2[K:K + 1, :].partition_broadcast(128),
                  inc=bcd)
            bc_here = bcd.v
            p.add("vector",
                  (lambda K=K, i=i: lambda e: e.tensor_mul(
                      yq[i][:], wfts[:, 256 * K:256 * K + 256], repA[i][:]))(),
                  waits=[(bcd, bc_here), (ld, ld_all)], inc=ved)
            p.add("gpsimd",
                  (lambda i=i: lambda e: e.tensor_copy(q8[i][:], yq[i][:]))(),
                  waits=[(ved, ved.v)], inc=ved)
            p.add("vector",
                  (lambda K=K, i=i: lambda e: e.tensor_mul(
                      wdq[:, 256 * K:256 * K + 256], q8[i][:], repB[i][:]))(),
                  waits=[(ved, ved.v)], inc=qd)
        p.dma("gpsimd", o_wdq.rearrange("(k q) n -> q k n", q=128),
              wdq.ap().rearrange("q (k n) -> q k n", k=16),
              waits=[(qd, qd.v)], inc=st)
        st_all = st.v
        p.add("gpsimd", lambda e: None if e.wait_ge(st.h, st_all) else None)

        p.emit(nc, ctx)
    return nc


def build_l2():
    nc = bass.Bass("TRN2", target_bir_lowering=False)
    x_fm = nc.dram_tensor("x_fm", [DM, NQ], F32, kind="ExternalInput").ap()
    wq_t = nc.dram_tensor("wq_t", [DM, 256], F32, kind="ExternalInput").ap()
    wk_t = nc.dram_tensor("wk_t", [DM, 256], F32, kind="ExternalInput").ap()
    wv_t = nc.dram_tensor("wv_t", [DM, 256], F32, kind="ExternalInput").ap()
    b_pm = nc.dram_tensor("b_pm", [128, 6], F32, kind="ExternalInput").ap()
    mod_all = nc.dram_tensor("mod_all", [128, 96], F32, kind="ExternalInput").ap()
    m8 = nc.dram_tensor("m8", [8, NT], F32, kind="ExternalInput").ap()
    v8 = nc.dram_tensor("v8", [8, NT], F32, kind="ExternalInput").ap()
    tri_in = nc.dram_tensor("tri_in", [128, 128], F32, kind="ExternalInput").ap()
    tris_in = nc.dram_tensor("tris_in", [128, 128], F32, kind="ExternalInput").ap()
    t01_in = nc.dram_tensor("t01_in", [128, 128], F32, kind="ExternalInput").ap()
    ts01_in = nc.dram_tensor("ts01_in", [128, 128], F32, kind="ExternalInput").ap()
    bc_in = nc.dram_tensor("bc_in", [128, 1], F32, kind="ExternalInput").ap()
    bcs_in = nc.dram_tensor("bcs_in", [128, 1], F32, kind="ExternalInput").ap()
    o_raw = nc.dram_tensor("o_raw", [256, NQ], BF16, kind="ExternalOutput").ap()
    o_invd = nc.dram_tensor("o_invd", [2, NQ], F32, kind="ExternalOutput").ap()
    scr_r = nc.dram_tensor("scr_r", [8, NT], F32, kind="Internal").ap()
    scr_mrn = nc.dram_tensor("scr_mrn", [8, NT], F32, kind="Internal").ap()
    ws_d = [nc.dram_tensor(f"ws_d{m}", [16, 128, 256], BF16, kind="Internal").ap()
            for m in range(3)]

    p = Prog()
    ld = p.sem("ld")
    prp = p.sem("prp")
    wl = p.sem("wl")
    wsd = p.sem("wsd")
    uvm = p.sem("uvm")
    rup = p.sem("rup")
    bcd = p.sem("bcd")
    xl = p.sem("xl")
    csd = p.sem("csd")
    wgl = p.sem("wgl")
    qmm = p.sem("qmm")
    epd = p.sem("epd")
    epa = p.sem("epa")
    vtd = p.sem("vtd")
    smm = p.sem("smm")
    trid = p.sem("trid")
    exd = p.sem("exd")
    pvm = p.sem("pvm")
    dnv = p.sem("dnv")
    oev = p.sem("oev")
    ost = p.sem("ost")
    dnm = p.sem("dnm")
    msk = p.sem("msk")
    st = p.sem("st")

    with ExitStack() as ctx:
        def sbt(name, shape, dtype=F32):
            return ctx.enter_context(nc.sbuf_tensor(name, shape, dtype))
        q_sb = [sbt(f"q_sb{h}", [128, NQ], BF16) for h in range(2)]
        k_sb = [sbt(f"k_sb{h}", [128, NQ], BF16) for h in range(2)]
        v_tm = [sbt(f"v_tm{h}", [128, NQ], BF16) for h in range(2)]
        ones_f32 = sbt("ones_f32", [128, 1])
        dd_c = sbt("dd_c", [1, NT])
        invd_c = sbt("invd_c", [1, NT])
        xb = [sbt(f"xb{k}", [128, NT], BF16) for k in range(16)]
        x32 = [sbt(f"x32_{i}", [128, NT]) for i in range(2)]
        r_rep = [sbt(f"r_rep{i}", [128, NT]) for i in range(2)]
        mrn_rep = [sbt(f"mrn_rep{i}", [128, NT]) for i in range(2)]
        wgrp = [sbt(f"wgrp{i}", [128, 16 * 128], BF16) for i in range(2)]
        w32 = [sbt(f"w32_{i}", [128, 256]) for i in range(2)]
        wsc = [sbt(f"wsc{i}", [128, 256], BF16) for i in range(2)]
        ep_t = [sbt(f"ep_t{i}", [128, NT]) for i in range(2)]
        vtmp = [sbt(f"vtmp{i}", [128, NT], BF16) for i in range(2)]
        ost_b = [sbt(f"ost_b{i}", [128, NT], BF16) for i in range(2)]
        p_bf = [sbt(f"p_bf{i}", [128, NT], BF16) for i in range(2)]
        pair = sbt("pair", [128, NT], BF16)
        acc = sbt("acc", [128, NT])
        mod_sb = sbt("mod_sb", [128, 96])
        onepca = sbt("onepca", [128, 16])
        one_t = sbt("one_t", [128, 1])
        b_sb = sbt("b_sb", [128, 6])
        u_sb = sbt("u_sb", [128, 6])
        v_sb = sbt("v_sb", [128, 6])
        tri = sbt("tri", [128, 128])
        tris = sbt("tris", [128, 128])
        tri01 = sbt("tri01", [128, 128])
        tris01 = sbt("tris01", [128, 128])
        biasC = sbt("biasC", [128, 1])
        biasCs = sbt("biasCs", [128, 1])
        m8s = sbt("m8s", [8, 512])
        v8s = sbt("v8s", [8, 512])
        r8x = sbt("r8x", [8, 512])
        epsb = sbt("epsb", [8, 1])
        ones_bf = sbt("ones_bf", [128, 1], BF16)

        S0 = ctx.enter_context(nc.psum_tensor("S0", [128, NT], F32))
        S1 = ctx.enter_context(nc.psum_tensor("S1", [128, NT], F32))
        O0 = ctx.enter_context(nc.psum_tensor("O0", [128, NT], F32))
        O1 = ctx.enter_context(nc.psum_tensor("O1", [128, NT], F32))
        SP = [S0, S1]
        OP = [O0, O1]

        p.dma("sync", mod_sb[:], mod_all, inc=ld)
        p.dma("sync", b_sb[:], b_pm, inc=ld)
        p.dma("sync", tri[:], tri_in, inc=ld)
        p.dma("sync", tris[:], tris_in, inc=ld)
        p.dma("sync", tri01[:], t01_in, inc=ld)
        p.dma("sync", tris01[:], ts01_in, inc=ld)
        p.dma("sync", biasC[:], bc_in, inc=ld)
        p.dma("sync", biasCs[:], bcs_in, inc=ld)
        ld_all = ld.v
        p.add("vector", lambda e: e.memset(ones_bf[:], 1.0), inc=prp)
        p.add("vector", lambda e: e.memset(ones_f32[:], 1.0), inc=prp)
        p.add("vector", lambda e: e.memset(epsb[:], EPS), inc=prp)
        p.add("vector", lambda e: e.memset(ones_bf[:], 1.0), inc=prp)
        p.add("vector", lambda e: e.memset(one_t[:], 1.0), inc=prp)
        p.add("scalar", lambda e: e.activation(
            out=onepca[:], in_=mod_sb[:, 16:32], func=AF.Identity,
            bias=one_t[:]),
            waits=[(ld, ld_all), (prp, prp.v)], inc=prp)
        onepca_ready = prp.v
        for hh in range(2):
            hsl = slice(512 * hh, 512 * hh + 512)
            p.dma("sync", m8s[:], m8[:, hsl], waits=[(rup, rup.v)], inc=ld)
            p.dma("sync", v8s[:], v8[:, hsl], inc=ld)
            p.add("scalar",
                  (lambda: lambda e: e.activation(
                      out=v8s[:], in_=v8s[:], func=AF.Ln, bias=epsb[:]))(),
                  waits=[(ld, ld.v), (prp, prp.v)], inc=prp)
            p.add("vector",
                  (lambda: lambda e: e.tensor_scalar(
                      m8s[:], m8s[:], -1.0, None, op0=ALU.mult))(),
                  waits=[(ld, ld.v)], inc=prp)
            p.add("scalar",
                  (lambda: lambda e: e.activation(
                      out=r8x[:], in_=v8s[:], func=AF.Exp, scale=-0.5))(),
                  waits=[(prp, prp.v)], inc=prp)
            p.add("gpsimd",
                  (lambda: lambda e: e.tensor_mul(m8s[:], m8s[:], r8x[:]))(),
                  waits=[(prp, prp.v)], inc=prp)
            p.dma("gpsimd", scr_r[:, hsl], r8x[:], waits=[(prp, prp.v)], inc=rup)
            p.dma("gpsimd", scr_mrn[:, hsl], m8s[:], inc=rup)
        rup_all = rup.v

        wts = [wq_t, wk_t, wv_t]
        first_uv = True
        for m in range(3):
            for f in range(16):
                i = (m * 16 + f) % 2
                wwait = ([(uvm, uvm.v), (wsd, wsd.v)] if (m * 16 + f) >= 2 else [])
                p.dma("sync", w32[i][:], wts[m][128 * f:128 * f + 128, :],
                      waits=wwait, inc=wl)
                p.add("vector",
                      (lambda i=i, f=f: lambda e: e.tensor_scalar(
                          wsc[i][:], w32[i][:], onepca[:, f:f + 1], None,
                          op0=ALU.mult))(),
                      waits=[(wl, wl.v), (prp, onepca_ready)], inc=prp)
                p.dma("gpsimd", ws_d[m].rearrange("a p n -> p a n")[:, f, :],
                      wsc[i].ap(), waits=[(prp, prp.v)], inc=wsd)
                for oc in range(2):
                    col = 2 * m + oc
                    p.add("tensor",
                          (lambda i=i, col=col, oc=oc, fu=first_uv,
                           lst=(m == 2 and f == 15 and oc == 1): lambda e: e.matmul(
                              S0[:, col:col + 1],
                              wsc[i][:, 128 * oc:128 * oc + 128], ones_bf[:],
                              start=fu, stop=lst))(),
                          waits=[(prp, prp.v)], inc=uvm)
                    first_uv = False
                    p.add("tensor",
                          (lambda i=i, col=col, oc=oc, m=m, f=f: lambda e: e.matmul(
                              S1[:, col:col + 1],
                              w32[i][:, 128 * oc:128 * oc + 128],
                              mod_sb[:, f:f + 1],
                              start=(m == 0 and f == 0 and oc == 0),
                              stop=(m == 2 and f == 15 and oc == 1)))(),
                          waits=[], inc=uvm)
        uv_all = uvm.v
        p.add("vector", lambda e: e.tensor_copy(u_sb[:], S0[:, 0:6]),
              waits=[(uvm, uv_all)], inc=prp)
        p.add("vector", lambda e: e.tensor_add(v_sb[:], S1[:, 0:6], b_sb[:]),
              inc=prp)
        uv_done = prp.v

        xb_free = [0] * 16
        x32_free = [0] * 2
        r_free = [0, 0]
        mrn_free = [0, 0]
        wg_free = [0, 0]
        ps_free = [0, 0, 0, 0]
        ep_free = [0, 0]
        vt_free = [0, 0]
        ost_free = [0, 0]
        for T in range(8):
            tp = T % 2
            p.dma("gpsimd", r_rep[tp][:],
                  scr_r[T:T + 1, :].partition_broadcast(128),
                  waits=[(rup, rup_all), (csd, r_free[tp])], inc=bcd)
            p.dma("gpsimd", mrn_rep[tp][:],
                  scr_mrn[T:T + 1, :].partition_broadcast(128),
                  waits=[(epd, mrn_free[tp])], inc=bcd)
            bc_here = bcd.v
            for K in range(16):
                xi = (T * 16 + K) % 2
                p.dma("sync", x32[xi][:],
                      x_fm.rearrange("(k q) n -> q k n", q=128)
                      [:, K, NT * T:NT * T + NT],
                      waits=[(csd, x32_free[xi])], inc=xl)
                p.add("vector",
                      (lambda K=K, xi=xi, tp=tp: lambda e: e.tensor_tensor(
                          xb[K][:], x32[xi][:], r_rep[tp][:], op=ALU.mult))(),
                      waits=[(xl, xl.v), (bcd, bc_here - 16),
                             (qmm, xb_free[K]), (prp, uv_done)], inc=csd)
                x32_free[xi] = csd.v
            csd_T = csd.v
            r_free[tp] = csd_T
            for o6 in range(6):
                m, oc = divmod(o6, 2)
                gi = o6 % 2
                psi = o6 % 4
                ps = (S0, S1, O0, O1)[psi]
                p.dma("sync", wgrp[gi].ap().rearrange("q (a n) -> q a n", a=16),
                      ws_d[m].rearrange("a p n -> p a n")
                      [:, :, 128 * oc:128 * oc + 128],
                      waits=[(wsd, wsd.v), (qmm, wg_free[gi])], inc=wgl)
                wg_here = wgl.v
                for f in range(16):
                    for half in range(2):
                        p.add("tensor",
                              (lambda ps=ps, f=f, half=half, gi=gi: lambda e:
                               e.matmul(
                                  ps[:, 512 * half:512 * half + 512],
                                  wgrp[gi][:, 128 * f:128 * f + 128],
                                  xb[f][:, 512 * half:512 * half + 512],
                                  start=(f == 0), stop=(f == 15)))(),
                              waits=([(csd, csd_T), (wgl, wg_here),
                                      (epd, ps_free[psi])]
                                     if f == 0 and half == 0 else []),
                              inc=qmm)
                wg_free[gi] = qmm.v
                for k in range(16):
                    xb_free[k] = qmm.v
                ei = o6 % 2
                p.add("vector",
                      (lambda ps=ps, ei=ei, tp=tp, col=o6: lambda e:
                       e.scalar_tensor_tensor(
                          ep_t[ei][:], in0=mrn_rep[tp][:],
                          scalar=u_sb[:, col:col + 1],
                          in1=ps[:], op0=ALU.mult, op1=ALU.add))(),
                      waits=[(qmm, qmm.v), (epa, ep_free[ei]),
                             (bcd, bc_here)], inc=epd)
                ps_free[psi] = epd.v
                if m < 2:
                    dst = (q_sb if m == 0 else k_sb)[oc]
                    p.add("scalar",
                          (lambda dst=dst, ei=ei, T=T, col=o6: lambda e:
                           e.activation(
                              out=dst[:, NT * T:NT * T + NT], in_=ep_t[ei][:],
                              func=AF.Identity, bias=v_sb[:, col:col + 1]))(),
                          waits=[(epd, epd.v)], inc=epa)
                    ep_free[ei] = epa.v
                else:
                    vi = oc
                    p.add("scalar",
                          (lambda ei=ei, vi=vi, col=o6: lambda e: e.activation(
                              out=vtmp[vi][:], in_=ep_t[ei][:],
                              func=AF.Identity, bias=v_sb[:, col:col + 1]))(),
                          waits=[(epd, epd.v), (vtd, vt_free[vi])], inc=epa)
                    ep_free[ei] = epa.v
                    for j in range(8):
                        p.dma("scalar",
                              v_tm[oc][:, NT * T + 128 * j:NT * T + 128 * j + 128],
                              vtmp[vi][:, 128 * j:128 * j + 128],
                              waits=([(epa, epa.v)] if j == 0 else []), inc=vtd,
                              transpose=True)
                    vt_free[vi] = vtd.v
            mrn_free[tp] = epd.v
        qkv_done_act = epa.v
        qkv_done_vt = vtd.v

        p_bf4 = [[xb[4 * h + j][:] for j in range(4)] for h in range(2)]
        acch = [ep_t[0], ep_t[1]]
        slots = []
        for t in range(8):
            i0 = NT * t
            bmin = max(0, 8 * t - 4)
            bs = list(range(63, bmin - 1, -1))
            if bmin > 0:
                bs.append(0)
            nb = len(bs)
            for idx, b in enumerate(bs):
                is_sink = (b == 0 and bmin > 0)
                is_t0b0 = (t == 0 and b == 0)
                npr = NT if (is_sink or is_t0b0) else min(NT, 128 * (b + 5) - i0)
                for h in range(2):
                    slots.append(dict(kind="blk", t=t, i0=i0, b=b, h=h, idx=idx,
                                      nb=nb, is_sink=is_sink, is_t0b0=is_t0b0,
                                      npr=npr,
                                      crossing=(not is_sink) and
                                      (128 * (b + 4) - i0 < NT)))
            slots.append(dict(kind="tile_end", t=t, i0=i0))

        s_free = [0, 0]
        pbf_free = {}
        o_ps_free = [0, 0]
        ost_free = [0, 0]
        sp_ctr = 0
        LAG = 2

        def emit_front(sl):
            nonlocal_ns = sl
            sp = sl["sp"]
            i0, b, h, npr = sl["i0"], sl["b"], sl["h"], sl["npr"]
            pieces = [(j * 512, min(512, npr - j * 512))
                      for j in range((npr + 511) // 512)]
            sl["pieces"] = pieces
            for pi, (c0, cw) in enumerate(pieces):
                p.add("tensor",
                      (lambda sp=sp, h=h, b=b, i0=i0, c0=c0, cw=cw:
                       lambda e: e.matmul(
                          SP[sp][:, c0:c0 + cw],
                          k_sb[h][:, 128 * b:128 * b + 128],
                          q_sb[h][:, i0 + c0:i0 + c0 + cw],
                          start=True, stop=True))(),
                      waits=([(exd, s_free[sp]), (epa, qkv_done_act)]
                             if pi == 0 else []),
                      inc=smm)
            if sl["crossing"]:
                col0 = 128 * (b + 4) - i0
                mt = tris if b == 0 else tri
                p.add("vector",
                      (lambda sp=sp, col0=col0, mt=mt: lambda e: e.tensor_add(
                          SP[sp][:, col0:col0 + 128],
                          SP[sp][:, col0:col0 + 128], mt[:]))(),
                      waits=[(smm, smm.v)], inc=trid)
            ewaits = ([(trid, trid.v)] if sl["crossing"] else [(smm, smm.v)])
            pbkey = (h, sl["idx"] % 4)
            sl["pbkey"] = pbkey
            pb = p_bf4[h][sl["idx"] % 4]
            sl["pb"] = pb
            pv_f, dn_f = pbf_free.get(pbkey, (0, 0))
            ewaits += [(pvm, pv_f), (dnv, dn_f)]
            if sl["is_t0b0"]:
                p.add("scalar",
                      (lambda sp=sp, pb=pb: lambda e: e.activation(
                          out=pb[:, 0:640], in_=SP[sp][:, 0:640],
                          func=AF.Exp, bias=biasC[:], scale=ASCALE))(),
                      waits=ewaits)
                p.add("scalar",
                      (lambda sp=sp, pb=pb: lambda e: e.activation(
                          out=pb[:, 640:NT], in_=SP[sp][:, 640:NT],
                          func=AF.Exp, bias=biasCs[:], scale=ASCALE))(),
                      inc=exd)
            else:
                bias_t = biasCs if sl["is_sink"] else biasC
                p.add("scalar",
                      (lambda sp=sp, pb=pb, npr=npr, bias_t=bias_t:
                       lambda e: e.activation(
                          out=pb[:, 0:npr], in_=SP[sp][:, 0:npr],
                          func=AF.Exp, bias=bias_t[:], scale=ASCALE))(),
                      waits=ewaits, inc=exd)
            sl["exd_v"] = exd.v
            s_free[sp] = exd.v
            if npr < NT:
                p.add("vector",
                      (lambda pb=pb, npr=npr: lambda e: e.memset(
                          pb[:, npr:NT], 0.0))(),
                      waits=[(exd, exd.v)], inc=dnv)
            sl["dnv_tail"] = dnv.v
            sl["msk_v"] = msk.v

        def emit_back(sl):
            h, b, idx, nb = sl["h"], sl["b"], sl["idx"], sl["nb"]
            pb = sl["pb"]
            for pi, (c0, cw) in enumerate(sl["pieces"]):
                p.add("tensor",
                      (lambda h=h, b=b, pb=pb, c0=c0, cw=cw,
                       stt=(idx == 0),
                       stp=(idx == nb - 1 and c0 + cw >= NT): lambda e:
                       e.matmul(
                          OP[h][:, c0:c0 + cw],
                          v_tm[h][:, 128 * b:128 * b + 128],
                          pb[:, c0:c0 + cw],
                          start=stt, stop=stp))(),
                      waits=(([(exd, sl["exd_v"]), (msk, sl["msk_v"]),
                               (vtd, qkv_done_vt)] +
                              ([(dnv, o_ps_free[h])] if idx == 0 else []))
                             if pi == 0 else []),
                      inc=pvm)
            if idx % 2 == 1:
                j0 = (idx - 1) % 4
                p.add("vector",
                      (lambda h=h, j0=j0: lambda e: e.tensor_add(
                          p_bf4[h][j0], p_bf4[h][j0], p_bf4[h][j0 + 1]))(),
                      waits=[(exd, sl["exd_v"]), (msk, sl["msk_v"]),
                             (dnv, sl["dnv_tail"]),
                             (pvm, pvm.v)], inc=dnv)
                p.add("vector",
                      (lambda h=h, j0=j0: lambda e: e.tensor_add(
                          acch[h][:], acch[h][:], p_bf4[h][j0]))(),
                      inc=dnv)
                pbf_free[(h, j0)] = (pvm.v, dnv.v)
                pbf_free[(h, j0 + 1)] = (pvm.v, dnv.v)
            elif idx == nb - 1:
                p.add("vector",
                      (lambda h=h, pb=pb: lambda e: e.tensor_add(
                          acch[h][:], acch[h][:], pb[:]))(),
                      waits=[(exd, sl["exd_v"]), (msk, sl["msk_v"]),
                             (dnv, sl["dnv_tail"]),
                             (dnv, dnv.v)], inc=dnv)
                pbf_free[sl["pbkey"]] = (pvm.v, dnv.v)

        def emit_tile_end(sl):
            i0 = sl["i0"]
            for h in range(2):
                p.add("vector",
                      (lambda h=h: lambda e: e.tensor_copy(
                          ost_b[h][:], OP[h][:]))(),
                      waits=[(pvm, pvm.v), (ost, ost_free[h])], inc=oev)
                p.dma("gpsimd", o_raw[128 * h:128 * h + 128, i0:i0 + NT],
                      ost_b[h].ap(), waits=[(oev, oev.v)], inc=ost)
                ost_free[h] = ost.v
                for half in range(2):
                    p.add("tensor",
                          (lambda h=h, half=half: lambda e: e.matmul(
                              OP[h][0:1, 512 * half:512 * half + 512],
                              ones_f32[:],
                              acch[h][:, 512 * half:512 * half + 512],
                              start=True, stop=True))(),
                          waits=([(oev, oev.v), (dnv, dnv.v)]
                                 if half == 0 else []),
                          inc=dnm)
                p.add("vector",
                      (lambda h=h: lambda e: e.tensor_copy(
                          dd_c[:], OP[h][0:1, :]))(),
                      waits=[(dnm, dnm.v)], inc=dnv)
                p.add("scalar", lambda e: e.activation(
                    out=dd_c[:], in_=dd_c[:], func=AF.Ln),
                    waits=[(dnv, dnv.v), (st, st.v)], inc=dnv)
                p.add("scalar", lambda e: e.activation(
                    out=invd_c[:], in_=dd_c[:], func=AF.Exp, scale=-1.0),
                    inc=dnv)
                p.dma("gpsimd", o_invd[h:h + 1, i0:i0 + NT], invd_c[:],
                      waits=[(dnv, dnv.v)], inc=st)
                o_ps_free[h] = dnv.v
                p.add("vector",
                      (lambda h=h: lambda e: e.memset(acch[h][:], 0.0))(),
                      waits=[(dnm, dnm.v), (epa, qkv_done_act)], inc=dnv)

        for h in range(2):
            p.add("vector",
                  (lambda h=h: lambda e: e.memset(acch[h][:], 0.0))(),
                  waits=[(epa, qkv_done_act)], inc=dnv)
        pend = []
        for sl in slots:
            if sl["kind"] == "tile_end":
                while pend:
                    emit_back(pend.pop(0))
                emit_tile_end(sl)
                continue
            sl["sp"] = sp_ctr % 2
            sp_ctr += 1
            emit_front(sl)
            pend.append(sl)
            if len(pend) > LAG:
                emit_back(pend.pop(0))
        while pend:
            emit_back(pend.pop(0))
        p.add("gpsimd", lambda e: e.wait_ge(st.h, st.v) and None)
        p.add("gpsimd", lambda e: e.wait_ge(ost.h, ost.v) and None)

        p.emit(nc, ctx)
    return nc


def build_l3():
    nc = bass.Bass("TRN2", target_bir_lowering=False)
    o_sl = nc.dram_tensor("o_sl", [DM, NT], BF16, kind="ExternalInput").ap()
    invd_sl = nc.dram_tensor("invd_sl", [16, NT], F32, kind="ExternalInput").ap()
    x_sl = nc.dram_tensor("x_sl", [DM, NT], F32, kind="ExternalInput").ap()
    wo_t = nc.dram_tensor("wo_t", [DM, DM], F32, kind="ExternalInput").ap()
    bo_pm = nc.dram_tensor("bo_pm", [128, 16], F32, kind="ExternalInput").ap()
    mod_all = nc.dram_tensor("mod_all", [128, 96], F32, kind="ExternalInput").ap()
    wdq_all = nc.dram_tensor("wdq_all", [DM, DM], BF16, kind="ExternalInput").ap()
    ident = nc.dram_tensor("ident", [128, 128], F32, kind="ExternalInput").ap()
    out_sl = nc.dram_tensor("out_sl", [DM, NT], F32, kind="ExternalOutput").ap()
    if L3_DEBUG:
        o_x1 = nc.dram_tensor("o_x1", [DM, NT], F32, kind="ExternalOutput").ap()
        o_h2q = nc.dram_tensor("o_h2q", [DM, NT], BF16, kind="ExternalOutput").ap()
        o_scl = nc.dram_tensor("o_scl", [8, DM], F32, kind="ExternalOutput").ap()
        o_g = nc.dram_tensor("o_g", [128, 32], F32, kind="ExternalOutput").ap()
    scr_r2 = nc.dram_tensor("scr_r2", [1, NT], F32, kind="Internal").ap()
    scr_m2 = nc.dram_tensor("scr_m2", [1, NT], F32, kind="Internal").ap()
    scr_rs = nc.dram_tensor("scr_rs", [128, 128], F32, kind="Internal").ap()
    scr_s2 = nc.dram_tensor("scr_s2", [128, 128], F32, kind="Internal").ap()
    h2_d = nc.dram_tensor("h2_d", [DM, NT], F32, kind="Internal").ap()

    p = Prog()
    ld = p.sem("ld")
    prp = p.sem("prp")
    ivb = p.sem("ivb")
    onr = p.sem("onr")
    wol = p.sem("wol")
    wob = p.sem("wob")
    omm = p.sem("omm")
    epa = p.sem("epa")
    epv = p.sem("epv")
    xld = p.sem("xld")
    sqd = p.sem("sqd")
    x1c = p.sem("x1c")
    smm = p.sem("smm")
    rwu = p.sem("rwu")
    rbc = p.sem("rbc")
    h2d = p.sem("h2d")
    trd = p.sem("trd")
    amx_s = p.sem("amx_s")
    qsu = p.sem("qsu")
    qbc = p.sem("qbc")
    qdd = p.sem("qdd")
    wfl = p.sem("wfl")
    fmm = p.sem("fmm")
    fea = p.sem("fea")
    fev = p.sem("fev")
    st = p.sem("st")

    with ExitStack() as ctx:
        def sbt(name, shape, dtype=F32):
            return ctx.enter_context(nc.sbuf_tensor(name, shape, dtype))
        o_sb = sbt("o_sb", [128, 16 * NT], BF16)
        x1 = sbt("x1", [128, 16 * NT])
        h2q = sbt("h2q", [128, 16 * NT], BF16)
        wo32 = [sbt(f"wo32_{i}", [128, 2 * NT]) for i in range(2)]
        wo_bf = [sbt(f"wo_bf{i}", [128, 2 * NT], BF16) for i in range(2)]
        wdqb = None
        wdqb = wo_bf
        inv_rep = [sbt(f"inv_rep{i}", [128, NT]) for i in range(2)]
        x32 = [sbt(f"x32_{i}", [128, NT]) for i in range(2)]
        ep = [sbt(f"ep{i}", [128, NT]) for i in range(2)]
        sqb = [sbt(f"sqb{i}", [128, NT], BF16) for i in range(2)]
        x1b = [sbt(f"x1b{i}", [128, NT], BF16) for i in range(2)]
        h2t = [sbt(f"h2t{i}", [128, NT]) for i in range(2)]
        th1 = x32
        rs_rep = [sbt(f"rs_rep{i}", [128, NT]) for i in range(2)]
        s2_rep = [wo32[0][:, 0:NT], wo32[1][:, 0:NT]]
        yq = ep
        q8 = [sbt(f"q8_{i}", [128, NT], FP8) for i in range(2)]
        out_sb = h2t
        amx_all = sbt("amx_all", [128, 128])
        rcp_all = sbt("rcp_all", [128, 128])
        s2_all = sbt("s2_all", [128, 128])
        mod_sb = sbt("mod_sb", [128, 96])
        ga_fm = sbt("ga_fm", [128, 16])
        gb_fm = sbt("gb_fm", [128, 16])
        gabo = sbt("gabo", [128, 16])
        cb1 = sbt("cb1", [128, 16])
        tsg = sbt("tsg", [128, 16])
        bo_sb = sbt("bo_sb", [128, 16])
        idt = sbt("idt", [128, 128])
        ones_bf = sbt("ones_bf", [128, 1], BF16)
        m2v = inv_rep[0][0:1, :]
        v2v = inv_rep[1][0:1, :]
        t2v = x32[0][0:1, :]
        r2x = x32[1][0:1, :]
        epsb = sbt("epsb", [1, 1])

        P0 = ctx.enter_context(nc.psum_tensor("P0", [128, NT], F32))
        P1 = ctx.enter_context(nc.psum_tensor("P1", [128, NT], F32))
        P2 = ctx.enter_context(nc.psum_tensor("P2", [128, NT], F32))
        P3 = ctx.enter_context(nc.psum_tensor("P3", [128, NT], F32))

        p.dma("sync", mod_sb[:], mod_all, inc=ld)
        p.dma("sync", bo_sb[:], bo_pm, inc=ld)
        p.dma("sync", idt[:], ident, inc=ld)
        p.dma("sync", o_sb.ap().rearrange("q (k n) -> q k n", k=16),
              o_sl.rearrange("(k q) n -> q k n", q=128), inc=ld)
        ld_all = ld.v
        p.add("vector", lambda e: e.memset(epsb[:], EPS), inc=prp)
        p.add("vector", lambda e: e.memset(ones_bf[:], 1.0), inc=prp)
        for (dst, col) in ((ga_fm, 32), (gb_fm, 80)):
            p.add("scalar",
                  (lambda dst=dst, col=col: lambda e: e.activation(
                      out=tsg[:], in_=mod_sb[:, col:col + 16], func=AF.Sign))(),
                  waits=[(ld, ld_all), (prp, prp.v)], inc=prp)
            p.add("scalar",
                  (lambda dst=dst, col=col: lambda e: e.activation(
                      out=dst[:], in_=mod_sb[:, col:col + 16], func=AF.Abs))(),
                  inc=prp)
            p.add("vector",
                  (lambda dst=dst: lambda e: e.tensor_scalar(
                      dst[:], dst[:], 0.25, None, op0=ALU.max))(),
                  waits=[(prp, prp.v)], inc=prp)
            p.add("gpsimd",
                  (lambda dst=dst: lambda e: e.tensor_mul(
                      dst[:], dst[:], tsg[:]))(),
                  waits=[(prp, prp.v)], inc=prp)
        p.add("gpsimd", lambda e: e.tensor_mul(gabo[:], ga_fm[:], bo_sb[:]),
              waits=[(prp, prp.v), (ld, ld_all)], inc=prp)
        p.add("vector", lambda e: e.tensor_scalar(
            cb1[:], mod_sb[:, 64:80], 1.0, None, op0=ALU.add),
            waits=[(ld, ld_all)], inc=prp)
        gates_done = prp.v

        iv_free = [0, 0]
        for K in range(16):
            ki = K % 2
            p.dma("gpsimd", inv_rep[ki][:],
                  invd_sl[K:K + 1, :].partition_broadcast(128),
                  waits=[(onr, iv_free[ki])], inc=ivb)
            p.add("vector",
                  (lambda K=K, ki=ki: lambda e: e.tensor_tensor(
                      o_sb[:, NT * K:NT * K + NT], o_sb[:, NT * K:NT * K + NT],
                      inv_rep[ki][:], op=ALU.mult))(),
                  waits=[(ivb, ivb.v), (ld, ld_all)], inc=onr)
            iv_free[ki] = onr.v
        onorm_done = onr.v
        wo_free = [0, 0]
        wob_free = [0, 0]
        ep_free = [0, 0]
        x32_free = [0, 0]
        ps_free = [0, 0]
        for of in range(16):
            oi = of % 2
            p.dma("sync", wo32[oi].ap().rearrange("q (k n) -> q k n", k=16),
                  wo_t.rearrange("(k q) n -> q k n", q=128)
                  [:, :, 128 * of:128 * of + 128],
                  waits=[(wob, wo_free[oi])], inc=wol)
            p.add("vector",
                  (lambda oi=oi: lambda e: e.tensor_copy(
                      wo_bf[oi][:], wo32[oi][:]))(),
                  waits=[(wol, wol.v), (omm, wob_free[oi])], inc=wob)
            p.dma("sync", x32[oi][:], x_sl[128 * of:128 * of + 128, :],
                  waits=[(epv, x32_free[oi])], inc=xld)
            ps = (P0, P1)[oi]
            for f in range(16):
                for half in range(2):
                    p.add("tensor",
                          (lambda ps=ps, oi=oi, f=f, half=half: lambda e:
                           e.matmul(
                              ps[:, 512 * half:512 * half + 512],
                              wo_bf[oi][:, 128 * f:128 * f + 128],
                              o_sb[:, NT * f + 512 * half:
                                   NT * f + 512 * half + 512],
                              start=(f == 0), stop=(f == 15)))(),
                          waits=([(wob, wob.v), (onr, onorm_done),
                                  (epa, ps_free[oi])]
                                 if f == 0 and half == 0 else []),
                          inc=omm)
            wob_free[oi] = omm.v
            p.add("scalar",
                  (lambda ps=ps, oi=oi, of=of: lambda e: e.activation(
                      out=ep[oi][:], in_=ps[:], func=AF.Identity,
                      scale=ga_fm[:, of:of + 1], bias=gabo[:, of:of + 1]))(),
                  waits=[(omm, omm.v), (prp, gates_done),
                         (epv, ep_free[oi])], inc=epa)
            ps_free[oi] = epa.v
            p.add("vector",
                  (lambda oi=oi, of=of: lambda e: e.tensor_add(
                      x1[:, NT * of:NT * of + NT], ep[oi][:], x32[oi][:]))(),
                  waits=[(epa, epa.v), (xld, xld.v)], inc=epv)
            ep_free[oi] = epv.v
            x32_free[oi] = epv.v
            wo_free[oi] = wob.v
        oproj_done = epv.v

        sq_free = [0, 0]
        x1b_free = [0, 0]
        for K in range(16):
            ki = K % 2
            p.add("scalar",
                  (lambda K=K, ki=ki: lambda e: e.activation(
                      out=sqb[ki][:], in_=x1[:, NT * K:NT * K + NT],
                      func=AF.Square))(),
                  waits=[(epv, oproj_done), (smm, sq_free[ki])], inc=sqd)
            p.add("vector",
                  (lambda K=K, ki=ki: lambda e: e.tensor_copy(
                      x1b[ki][:], x1[:, NT * K:NT * K + NT]))(),
                  waits=[(epv, oproj_done), (smm, x1b_free[ki])], inc=x1c)
            for half in range(2):
                p.add("tensor",
                      (lambda K=K, ki=ki, half=half: lambda e: e.matmul(
                          P2[0:1, 512 * half:512 * half + 512],
                          ones_bf[:],
                          x1b[ki][:, 512 * half:512 * half + 512],
                          start=(K == 0), stop=(K == 15)))(),
                      waits=[(x1c, x1c.v), (prp, 2)] if half == 0 else [],
                      inc=smm)
                p.add("tensor",
                      (lambda K=K, ki=ki, half=half: lambda e: e.matmul(
                          P3[0:1, 512 * half:512 * half + 512],
                          ones_bf[:],
                          sqb[ki][:, 512 * half:512 * half + 512],
                          start=(K == 0), stop=(K == 15)))(),
                      waits=[(sqd, sqd.v)] if half == 0 else [], inc=smm)
            sq_free[ki] = smm.v
            x1b_free[ki] = smm.v
        smm_all = smm.v
        p.add("vector", lambda e: e.tensor_scalar(
            m2v, P2[0:1, :], 1.0 / DM, None, op0=ALU.mult),
            waits=[(smm, smm_all)], inc=prp)
        p.add("vector", lambda e: e.tensor_scalar(
            v2v, P3[0:1, :], 1.0 / DM, None, op0=ALU.mult), inc=prp)
        p.add("vector", lambda e: e.tensor_mul(t2v, m2v, m2v),
              waits=[(prp, prp.v)], inc=prp)
        p.add("vector", lambda e: e.memset(sqb[0][:], 0.0), inc=prp)
        p.add("vector", lambda e: e.tensor_sub(v2v, v2v, t2v), inc=prp)
        p.add("scalar", lambda e: e.activation(
            out=v2v, in_=v2v, func=AF.Ln, bias=epsb[:]),
            waits=[(prp, prp.v)], inc=prp)
        p.add("vector", lambda e: e.tensor_scalar(
            m2v, m2v, -1.0, None, op0=ALU.mult), inc=prp)
        p.add("scalar", lambda e: e.activation(
            out=r2x, in_=v2v, func=AF.Exp, scale=-0.5),
            waits=[(prp, prp.v)], inc=prp)
        p.add("gpsimd", lambda e: e.tensor_mul(m2v, m2v, r2x),
            waits=[(prp, prp.v)], inc=prp)
        rows2_done = prp.v
        p.dma("gpsimd", scr_r2, r2x, waits=[(prp, rows2_done)], inc=rwu)
        p.dma("gpsimd", scr_m2, m2v, inc=rwu)
        rwu_all = rwu.v
        p.dma("gpsimd", rs_rep[0][:], scr_r2.partition_broadcast(128),
              waits=[(rwu, rwu_all)], inc=rbc)
        p.dma("gpsimd", rs_rep[1][:], scr_m2.partition_broadcast(128), inc=rbc)
        r2bc_done = rbc.v
        r2_rep = rs_rep[0]
        m2n_rep = rs_rep[1]

        h2w = p.sem("h2w")
        th_free = [0, 0]
        h2_free = [0, 0]
        amx_war = trd.v
        for K in range(16):
            ki = K % 2
            p.add("vector",
                  (lambda K=K, ki=ki: lambda e: e.tensor_tensor(
                      th1[ki][:], x1[:, NT * K:NT * K + NT], r2_rep[:],
                      op=ALU.mult))(),
                  waits=[(rbc, r2bc_done), (h2d, th_free[ki])], inc=h2d)
            p.add("vector",
                  (lambda ki=ki: lambda e: e.scalar_tensor_tensor(
                      th1[ki][:], in0=m2n_rep[:], scalar=1.0, in1=th1[ki][:],
                      op0=ALU.mult, op1=ALU.add))(),
                  inc=h2d)
            p.add("scalar",
                  (lambda K=K, ki=ki: lambda e: e.activation(
                      out=h2t[ki][:], in_=th1[ki][:], func=AF.Identity,
                      scale=cb1[:, K:K + 1],
                      bias=mod_sb[:, 48 + K:48 + K + 1]))(),
                  waits=[(h2d, h2d.v), (h2w, h2_free[ki])], inc=h2d)
            th_free[ki] = h2d.v
            p.dma("gpsimd", h2_d[128 * K:128 * K + 128, :], h2t[ki].ap(),
                  waits=[(h2d, h2d.v)], inc=h2w)
            h2_free[ki] = h2w.v
            for j in range(8):
                p.add("tensor",
                      (lambda ki=ki, j=j: lambda e: e.transpose(
                          P2[:, 128 * j:128 * j + 128],
                          h2t[ki][:, 128 * j:128 * j + 128], idt[:]))(),
                      waits=([(h2d, h2d.v), (ld, ld_all), (amx_s, amx_war)]
                             if j == 0 else []),
                      inc=trd)
            p.add("vector",
                  (lambda K=K: lambda e: e.tensor_reduce(
                      amx_all[:, 8 * K:8 * K + 8],
                      P2.ap().rearrange("q (a n) -> q a n", a=8),
                      axis=AX.X, op=ALU.max, apply_absolute_value=True))(),
                  waits=[(trd, trd.v)], inc=amx_s)
            amx_war = amx_s.v
        p.add("scalar", lambda e: e.activation(
            out=rcp_all[:], in_=amx_all[:], func=AF.Ln, scale=1.0 / 224.0),
            waits=[(amx_s, amx_s.v)], inc=amx_s)
        p.add("scalar", lambda e: e.activation(
            out=s2_all[:], in_=rcp_all[:], func=AF.Exp, scale=1.0),
            inc=amx_s)
        p.add("scalar", lambda e: e.activation(
            out=rcp_all[:], in_=rcp_all[:], func=AF.Exp, scale=-1.0),
            inc=amx_s)
        p.add("tensor", lambda e: e.transpose(
            P2[:, 0:128], rcp_all[:], idt[:]),
            waits=[(amx_s, amx_s.v)], inc=trd)
        p.add("tensor", lambda e: e.transpose(
            P2[:, 512:640], s2_all[:], idt[:]), inc=trd)
        p.add("vector", lambda e: e.tensor_copy(rcp_all[:], P2[:, 0:128]),
              waits=[(trd, trd.v)], inc=amx_s)
        p.add("vector", lambda e: e.tensor_copy(s2_all[:], P2[:, 512:640]),
              inc=amx_s)
        p.dma("gpsimd", scr_rs, rcp_all[:], waits=[(amx_s, amx_s.v)], inc=qsu)
        p.dma("gpsimd", scr_s2, s2_all[:], inc=qsu)
        qsu_all = qsu.v
        qrep_free = [0, 0]
        h2l_free = [0, 0]
        for K in range(16):
            ki = K % 2
            p.dma("sync", th1[ki][:], h2_d[128 * K:128 * K + 128, :],
                  waits=[(h2w, h2w.v), (qdd, h2l_free[ki])], inc=qbc)
            p.dma("sync", yq[ki][:],
                  scr_rs[8 * K:8 * K + 8, :].partition_broadcast(128),
                  waits=[(qsu, qsu_all), (qdd, qrep_free[ki])], inc=qbc)
            p.dma("sync", s2_rep[ki],
                  scr_s2[8 * K:8 * K + 8, :].partition_broadcast(128),
                  inc=qbc)
            qbc_here = qbc.v
            p.add("vector",
                  (lambda ki=ki: lambda e: e.tensor_tensor(
                      yq[ki][:], th1[ki][:], yq[ki][:], op=ALU.mult))(),
                  waits=[(qbc, qbc_here)], inc=qdd)
            p.add("vector",
                  (lambda ki=ki: lambda e: e.tensor_copy(q8[ki][:], yq[ki][:]))(),
                  inc=qdd)
            p.add("vector",
                  (lambda K=K, ki=ki: lambda e: e.tensor_tensor(
                      h2q[:, NT * K:NT * K + NT], q8[ki][:], s2_rep[ki],
                      op=ALU.mult))(),
                  inc=qdd)
            h2l_free[ki] = qdd.v
            qrep_free[ki] = qdd.v
        quant_done = qdd.v
        if L3_DEBUG:
            p.dma("gpsimd", o_g[:, 0:16], ga_fm.ap(), waits=[(prp, gates_done)],
                  inc=st)
            p.dma("gpsimd", o_g[:, 16:32], gb_fm.ap(), inc=st)
            p.dma("gpsimd", o_x1.rearrange("(k q) n -> q k n", q=128),
                  x1.ap().rearrange("q (k n) -> q k n", k=16),
                  waits=[(epv, oproj_done)], inc=st)
            p.dma("gpsimd", o_h2q.rearrange("(k q) n -> q k n", q=128),
                  h2q.ap().rearrange("q (k n) -> q k n", k=16),
                  waits=[(qdd, quant_done)], inc=st)
            p.dma("gpsimd", o_scl, scr_rs, waits=[(qsu, qsu.v)], inc=st)

        wf_free = [0, 0]
        fps_free = [0, 0]
        fep_free = [0, 0]
        fout_free = [0, 0]
        for of in range(16):
            oi = of % 2
            p.dma("sync", wdqb[oi].ap().rearrange("q (k n) -> q k n", k=16),
                  wdq_all.rearrange("(k q) n -> q k n", q=128)
                  [:, :, 128 * of:128 * of + 128],
                  waits=[(fmm, wf_free[oi])], inc=wfl)
            wfl_here = wfl.v
            ps = (P0, P1)[oi]
            for f in range(16):
                for half in range(2):
                    p.add("tensor",
                          (lambda ps=ps, oi=oi, f=f, half=half: lambda e:
                           e.matmul(
                              ps[:, 512 * half:512 * half + 512],
                              wdqb[oi][:, 128 * f:128 * f + 128],
                              h2q[:, NT * f + 512 * half:
                                   NT * f + 512 * half + 512],
                              start=(f == 0), stop=(f == 15)))(),
                          waits=([(wfl, wfl_here), (qdd, quant_done),
                                  (fea, fps_free[oi])]
                                 if f == 0 and half == 0 else []),
                          inc=fmm)
            wf_free[oi] = fmm.v
            p.add("scalar",
                  (lambda ps=ps, oi=oi, of=of: lambda e: e.activation(
                      out=ep[oi][:], in_=ps[:], func=AF.Identity,
                      scale=gb_fm[:, of:of + 1]))(),
                  waits=[(fmm, fmm.v), (fev, fep_free[oi])], inc=fea)
            fps_free[oi] = fea.v
            p.add("vector",
                  (lambda oi=oi, of=of: lambda e: e.tensor_add(
                      out_sb[oi][:], ep[oi][:],
                      x1[:, NT * of:NT * of + NT]))(),
                  waits=[(fea, fea.v), (st, fout_free[oi])], inc=fev)
            fep_free[oi] = fev.v
            p.dma("gpsimd", out_sl[128 * of:128 * of + 128, :],
                  out_sb[oi].ap(), waits=[(fev, fev.v)], inc=st)
            fout_free[oi] = st.v
        p.add("gpsimd", lambda e: e.wait_ge(st.h, st.v) and None)

        p.emit(nc, ctx)
    return nc


def _bf(x):
    return np.ascontiguousarray(x, dtype=ml_dtypes.bfloat16)


def _f32(x):
    return np.ascontiguousarray(x, dtype=np.float32)


_NC_CACHE = {}


def _get_nc(name):
    if name not in _NC_CACHE:
        _NC_CACHE[name] = {"l1": build_l1, "l2": build_l2, "l3": build_l3}[name]()
    return _NC_CACHE[name]


def kernel(**inputs):
    x = np.asarray(inputs["L0_attn_hidden_states"])[0]
    x_fm = np.ascontiguousarray(x.T)
    cvec = np.ascontiguousarray(np.asarray(inputs["L0_c"]))
    wmod = np.asarray(inputs["L0_wmod"])
    wffn = np.asarray(inputs["L0_ffn_weight"])
    wffn_t = np.ascontiguousarray(wffn.T)
    Wq_t = np.ascontiguousarray(np.asarray(inputs["L0_attn_Wq"]).T)
    Wk_t = np.ascontiguousarray(np.asarray(inputs["L0_attn_Wk"]).T)
    Wv_t = np.ascontiguousarray(np.asarray(inputs["L0_attn_Wv"]).T)
    Wo_t = np.ascontiguousarray(np.asarray(inputs["L0_attn_Wo"]).T)
    bq = np.asarray(inputs["L0_attn_bq"])
    bk = np.asarray(inputs["L0_attn_bk"])
    bv = np.asarray(inputs["L0_attn_bv"])
    bo = np.asarray(inputs["L0_attn_bo"])
    ident = np.eye(128, dtype=np.float32)
    cores = list(range(NCORE))

    im1 = []
    for c in cores:
        im1.append({
            "x_sl": np.ascontiguousarray(x_fm[:, NT * c:NT * (c + 1)]),
            "c_in": cvec,
            "wmod_sl": np.ascontiguousarray(wmod[:, 1536 * c:1536 * (c + 1)]),
            "wfn": np.ascontiguousarray(wffn[256 * c:256 * (c + 1), :]),
            "wft": np.ascontiguousarray(wffn_t[:, 256 * c:256 * (c + 1)]),
            "ident": ident,
        })
    r1 = run_bass_kernel_spmd(_get_nc("l1"), im1, core_ids=cores).results

    mod_all = np.concatenate([r1[c]["o_mod"] for c in cores], axis=1)
    m8 = np.stack([r1[c]["o_stats"][0] for c in cores], axis=0)
    v8 = np.stack([r1[c]["o_stats"][1] for c in cores], axis=0)
    wdq_all = np.concatenate([r1[c]["o_wdq"] for c in cores], axis=1)

    jidx = np.arange(128)
    tri = np.where(jidx[:, None] < jidx[None, :], np.float32(NEG),
                   np.float32(0)).astype(np.float32)
    tris = np.where((jidx[:, None] >= SINKS) & (jidx[:, None] < jidx[None, :]),
                    np.float32(NEG), np.float32(0)).astype(np.float32)
    bc = np.full((128, 1), -CSH, dtype=np.float32)
    bcs = bc.copy()
    bcs[SINKS:, 0] = -1.0e38
    t01 = np.where(jidx[:, None] < jidx[None, :], np.float32(0),
                   np.float32(1)).astype(np.float32)
    ts01 = np.where((jidx[:, None] >= SINKS) & (jidx[:, None] < jidx[None, :]),
                    np.float32(0), np.float32(1)).astype(np.float32)
    im2 = []
    for c in cores:
        sl = slice(256 * c, 256 * c + 256)
        b_pm = np.stack([bq[sl].reshape(2, 128).T, bk[sl].reshape(2, 128).T,
                         bv[sl].reshape(2, 128).T], axis=1).reshape(128, 6)
        im2.append({
            "x_fm": x_fm,
            "wq_t": np.ascontiguousarray(Wq_t[:, sl]),
            "wk_t": np.ascontiguousarray(Wk_t[:, sl]),
            "wv_t": np.ascontiguousarray(Wv_t[:, sl]),
            "b_pm": np.ascontiguousarray(b_pm, dtype=np.float32),
            "mod_all": mod_all, "m8": m8, "v8": v8,
            "tri_in": tri, "tris_in": tris, "bc_in": bc, "bcs_in": bcs,
            "t01_in": t01, "ts01_in": ts01,
        })
    r2 = run_bass_kernel_spmd(_get_nc("l2"), im2, core_ids=cores).results

    o_all = np.concatenate([r2[c]["o_raw"] for c in cores], axis=0)
    invd_all = np.concatenate([r2[c]["o_invd"] for c in cores], axis=0)

    bo_pm = np.ascontiguousarray(bo.reshape(16, 128).T, dtype=np.float32)
    im3 = []
    for c in cores:
        tsl = slice(NT * c, NT * (c + 1))
        im3.append({
            "o_sl": np.ascontiguousarray(o_all[:, tsl]),
            "invd_sl": np.ascontiguousarray(invd_all[:, tsl]),
            "x_sl": np.ascontiguousarray(x_fm[:, tsl]),
            "wo_t": Wo_t,
            "bo_pm": bo_pm,
            "mod_all": mod_all,
            "wdq_all": wdq_all,
            "ident": ident,
        })
    r3 = run_bass_kernel_spmd(_get_nc("l3"), im3, core_ids=cores).results
    out_fm = np.concatenate([r3[c]["out_sl"] for c in cores], axis=1)
    return np.ascontiguousarray(out_fm.T)[None, :, :]
